# version 17
# baseline (speedup 1.0000x reference)
import numpy as np
import ml_dtypes
from contextlib import ExitStack

import concourse.bass as bass
import concourse.mybir as mybir
from concourse.bass_utils import run_bass_kernel_spmd

F32 = mybir.dt.float32
BF16 = mybir.dt.bfloat16
AF = mybir.ActivationFunctionType
ALU = mybir.AluOpType
AX = mybir.AxisListType.X

SHIFT = 12.0
LN_EPS = 1e-5
D = 1024
CC = D // 128
E = 8


def _fm(ap2d, c0, cn):
    return ap2d[:, c0:c0 + cn].rearrange("(cc p) k -> p cc k", p=128)


def _tm(ap2d, r0, nt):
    return ap2d[r0:r0 + nt * 128, :].rearrange("(t p) c -> p t c", p=128)


def build(seq=8192, H=4096):
    assert seq % 2048 == 0 and H % 512 == 0
    GA = seq // 256
    GB = seq // 256
    NKC = seq // 128
    NCH = seq // 512
    NT = seq // 128
    TQ = 2048
    NQ = seq // TQ
    HQ = H // 4
    HT = HQ // 128
    HTT = H // 128
    CAP = seq // 4 + 512
    NSLOT = E * CAP
    STE = CAP // 128
    SGE = CAP // 512
    BIGS = 1.0e8

    nc = bass.Bass("TRN2", target_bir_lowering=False)

    x_d = nc.dram_tensor("L0_attn_x", [seq, D], F32, kind="ExternalInput").ap()
    wqkv_d = nc.dram_tensor("L0_attn_weight_qkv", [D, 3 * D], F32, kind="ExternalInput").ap()
    wo_d = nc.dram_tensor("L0_attn_out_w", [D, D], F32, kind="ExternalInput").ap()
    ob_d = nc.dram_tensor("L0_attn_out_b", [1, D], F32, kind="ExternalInput").ap()
    rw_d = nc.dram_tensor("L0_ffn_router_w", [E, D], F32, kind="ExternalInput").ap()
    wfc_d = nc.dram_tensor("L0_ffn_w_fc", [E * D, H], F32, kind="ExternalInput").ap()
    bfc_d = nc.dram_tensor("L0_ffn_b_fc", [E, H], F32, kind="ExternalInput").ap()
    wpj_d = nc.dram_tensor("L0_ffn_w_proj", [E * H, D], F32, kind="ExternalInput").ap()
    pck_h = nc.dram_tensor("pck_h", [8 * (seq // 4 + 512) + seq // 128, 2], F32, kind="Internal").ap()
    bpj_d = nc.dram_tensor("L0_ffn_b_proj", [E, D], F32, kind="ExternalInput").ap()
    idb_d = nc.dram_tensor("ident_bf", [128, 128], BF16, kind="ExternalInput").ap()
    idf_d = nc.dram_tensor("ident_f32", [128, 128], F32, kind="ExternalInput").ap()

    out_d = nc.dram_tensor("out", [seq, D], F32, kind="ExternalOutput").ap()

    qT_h = nc.dram_tensor("qT_h", [D, seq], BF16, kind="Internal").ap()
    kT_h = nc.dram_tensor("kT_h", [D, seq], BF16, kind="Internal").ap()
    y_h = nc.dram_tensor("y_h", [seq, D], BF16, kind="Internal").ap()
    x1b_h = nc.dram_tensor("x1b_h", [seq + 128, D], BF16, kind="Internal").ap()
    yin_h = nc.dram_tensor("yin_h", [seq + 128, D], F32, kind="Internal").ap()

    es = ExitStack()
    with es:
        sem = {}
        for name in ("c0 consts "
                     "ax apet apeq apev aevq aevt aevv asp "
                     "bq bk bpes bexp bpey bdvey brs0 bself bsp "
                     "cy cxr cpet cdvet cpeao cvar crstd cx1 cx1bf ct8 ce8 "
                     "ccomb ccombTe cper cpecb cpeb cinitd cx1te cspx cspi cv ca "
                     "dw dx1 dacc dpeh dact dpep ddve dvar drstd dfin dout dv da "
                     "ax1 axw asp1 bq1 bk1 bsp1 cy1 cxr1 cspx1 cspi1 dw1 dout1 cwo "
                     "rv rpe rst rst1 rpk rio dic dgt dgv dsca dstg dpk dpk1 dg dg1").split():
            sem[name] = es.enter_context(nc.semaphore(name))

        def P2(name, i):
            return sem[name] if i % 2 == 0 else sem[name + "1"]

        idb = es.enter_context(nc.sbuf_tensor("idb", [128, 128], BF16))
        idf = es.enter_context(nc.sbuf_tensor("idf", [128, 128], F32))
        RwT = es.enter_context(nc.sbuf_tensor("RwT", [128, CC * 8], BF16))
        bfc_s = es.enter_context(nc.sbuf_tensor("bfc_s", [128, HTT * E], F32))
        bpj_s = es.enter_context(nc.sbuf_tensor("bpj_s", [8, D], BF16))
        obr = es.enter_context(nc.sbuf_tensor("obr", [128, D], F32))
        ones_b = es.enter_context(nc.sbuf_tensor("ones_b", [128, 1], BF16))
        ones_f = es.enter_context(nc.sbuf_tensor("ones_f", [1, 128], F32))
        cns = es.enter_context(nc.sbuf_tensor("cns", [128, 2], F32))
        rs_all = es.enter_context(nc.sbuf_tensor("rs_all", [128, 2 * GB], F32))
        esAB = es.enter_context(ExitStack())
        v_sb = esAB.enter_context(nc.sbuf_tensor("v_sb", [128, NKC * D], BF16))

        with ExitStack() as ps0:
            ec = ps0.enter_context
            rw_nat = ec(nc.sbuf_tensor("rw_nat", [8, D], BF16))
            bfc_nat = ec(nc.sbuf_tensor("bfc_nat", [8, H], F32))
            ob_row = ec(nc.sbuf_tensor("ob_row", [1, D], F32))
            bpj_nat = ec(nc.sbuf_tensor("bpj_nat", [8, D], F32))
            p0b = ec(nc.psum_tensor("p0b", [128, 1024], BF16))
            p0b2 = ec(nc.psum_tensor("p0b2", [128, 1024], BF16))
            p0f = ec(nc.psum_tensor("p0f", [128, 512], F32))
            p0f2 = ec(nc.psum_tensor("p0f2", [128, 512], F32))
            P0B = [p0b, p0b2]
            P0F = [p0f, p0f2]

            N_P0DMA = 6

            with nc.Block() as blk:
                @blk.gpsimd
                def _(g):
                    n0 = 0

                    def dm(out, in_):
                        nonlocal n0
                        g.dma_start(out=out, in_=in_).then_inc(sem["c0"], 16)
                        n0 += 1
                        g.wait_ge(sem["c0"], 16 * n0)

                    dm(idb[:], idb_d)
                    dm(idf[:], idf_d)
                    dm(rw_nat[:], rw_d)
                    dm(bfc_nat[:], bfc_d)
                    dm(ob_row[:], ob_d)
                    dm(bpj_nat[:], bpj_d)

                @blk.tensor
                def _(t):
                    t.wait_ge(sem["c0"], 16 * 3)
                    jb = 0
                    for cc in range(CC):
                        if jb >= 2:
                            t.wait_ge(sem["consts"], jb + 2)
                        t.transpose(P0B[jb % 2][0:128, 0:8],
                                    rw_nat[0:8, cc * 128: cc * 128 + 128],
                                    idb[0:8, 0:8]).then_inc(sem["apet"], 1)
                        jb += 1
                    NB = jb
                    t.wait_ge(sem["c0"], 16 * 4)
                    jf = 0
                    for ht in range(HTT):
                        if jf >= 2:
                            t.wait_ge(sem["consts"], NB + jf + 2)
                        t.transpose(P0F[jf % 2][0:128, 0:8],
                                    bfc_nat[0:8, ht * 128: ht * 128 + 128],
                                    idf[0:8, 0:8]).then_inc(sem["apet"], 1)
                        jf += 1
                    t.wait_ge(sem["c0"], 16 * 5)
                    t.wait_ge(sem["aevq"], 2)
                    for h in range(2):
                        if jf >= 2:
                            t.wait_ge(sem["consts"], NB + jf + 2)
                        t.matmul(P0F[jf % 2][:, 0:512], ones_f[0:1, 0:128],
                                 ob_row[0:1, h * 512:(h + 1) * 512],
                                 start=True, stop=True).then_inc(sem["apet"], 1)
                        jf += 1

                @blk.vector
                def _(v):
                    v.memset(ones_b[:], 1.0).then_inc(sem["aevq"], 1)
                    v.memset(ones_f[:], 1.0).then_inc(sem["aevq"], 1)
                    v.memset(cns[:, 0:1], -SHIFT).then_inc(sem["consts"], 1)
                    v.memset(cns[:, 1:2], LN_EPS).then_inc(sem["consts"], 1)
                    v.wait_ge(sem["c0"], 16 * 6)
                    v.tensor_copy(bpj_s[:], bpj_nat[:]).then_inc(sem["consts"], 1)
                    n = 1
                    jb = 0
                    for cc in range(CC):
                        v.wait_ge(sem["apet"], jb + 1)
                        v.tensor_copy(RwT[:, cc * 8: cc * 8 + 8],
                                      P0B[jb % 2][0:128, 0:8]).then_inc(sem["consts"], 1)
                        jb += 1
                        n += 1
                    jf = 0
                    for ht in range(HTT):
                        v.wait_ge(sem["apet"], jb + jf + 1)
                        v.tensor_copy(bfc_s[:, ht * 8: ht * 8 + 8],
                                      P0F[jf % 2][0:128, 0:8]).then_inc(sem["consts"], 1)
                        jf += 1
                        n += 1
                    for h in range(2):
                        v.wait_ge(sem["apet"], jb + jf + 1)
                        v.tensor_copy(obr[:, h * 512:(h + 1) * 512],
                                      P0F[jf % 2][:, 0:512]).then_inc(sem["consts"], 1)
                        jf += 1
                        n += 1
            CONSTS = 2 + 1 + CC + HTT + 2
            P0_PET = CC + HTT + 2

        with ExitStack() as psA:
            ec = psA.enter_context
            wqkv_s = ec(nc.sbuf_tensor("wqkv_s", [128, CC * 3 * D], BF16))
            xbf = ec(nc.sbuf_tensor("xbf", [128, 2 * 2 * D], BF16))
            xT = ec(nc.sbuf_tensor("xT", [128, 2 * CC * 256], BF16))
            qks = ec(nc.sbuf_tensor("qks", [128, 2 * 512], BF16))
            pat = ec(nc.psum_tensor("pat", [128, 1024], BF16))
            pat2 = ec(nc.psum_tensor("pat2", [128, 1024], BF16))
            paq = ec(nc.psum_tensor("paq", [128, 512], F32))
            paq2 = ec(nc.psum_tensor("paq2", [128, 512], F32))
            pav = ec(nc.psum_tensor("pav", [128, 512], F32))
            pav2 = ec(nc.psum_tensor("pav2", [128, 512], F32))
            blk = ec(nc.Block())
            PAT = [pat, pat2]
            PAQ = [paq, paq2]
            PAV = [pav, pav2]

            @blk.gpsimd
            def _(g):
                g.wait_ge(sem["consts"], CONSTS)
                g.dma_start(out=wqkv_s[:].rearrange("p (cc n) -> p cc n", cc=CC),
                            in_=_fm(wqkv_d, 0, 3 * D)).then_inc(sem["axw"], 16)
                for ga in range(GA):
                    if ga >= 2:
                        g.wait_ge(sem["apet"], P0_PET + (ga - 1) * 16)
                    g.dma_start(out=xbf[:, (ga % 2) * 2 * D:(ga % 2) * 2 * D + 2 * D]
                                .rearrange("p (t c) -> p t c", t=2),
                                in_=_tm(x_d, ga * 256, 2)).then_inc(P2("ax", ga), 16)

            @blk.sync
            def _(q):
                for ga in range(GA):
                    for m in range(16):
                        k = ga * 16 + m
                        q.wait_ge(sem["aevq"], 2 + k + 1)
                        dst = qT_h if m < 8 else kT_h
                        r0 = (m % 8) * 128
                        q.dma_start(out=dst[r0:r0 + 128, ga * 256:(ga + 1) * 256]
                                    .rearrange("p (u k) -> p u k", u=1),
                                    in_=qks[:, (k % 2) * 512:(k % 2) * 512 + 256]
                                    .rearrange("p (u k) -> p u k", u=1)).then_inc(P2("asp", k), 16)

            @blk.tensor
            def _(t):
                t.wait_ge(sem["consts"], CONSTS)
                if True:
                    t.wait_ge(sem["axw"], 16)
                for ga in range(GA):
                    t.wait_ge(P2("ax", ga), 16 * (ga // 2 + 1))
                    for tt in range(2):
                        for cc in range(CC):
                            j = ga * 16 + tt * 8 + cc
                            if j >= 2:
                                t.wait_ge(sem["aevt"], j - 1)
                            t.transpose(PAT[j % 2][:, 0:128],
                                        xbf[:, (ga % 2) * 2 * D + tt * D + cc * 128:
                                            (ga % 2) * 2 * D + tt * D + cc * 128 + 128],
                                        idb[:]).then_inc(sem["apet"], 1)
                    t.wait_ge(sem["aevt"], (ga + 1) * 16)
                    for m in range(16):
                        k = ga * 16 + m
                        if k >= 2:
                            t.wait_ge(sem["aevq"], 2 + k - 1)
                        col0 = m * 128 if m < 8 else D + (m - 8) * 128
                        for cc in range(CC):
                            mm = t.matmul(PAQ[k % 2][:, 0:256],
                                          wqkv_s[:, cc * 3 * D + col0: cc * 3 * D + col0 + 128],
                                          xT[:, (ga % 2) * CC * 256 + cc * 256:
                                              (ga % 2) * CC * 256 + cc * 256 + 256],
                                          start=(cc == 0), stop=(cc == CC - 1))
                        mm.then_inc(sem["apeq"], 1)
                    for tt in range(2):
                        for ch in range(2):
                            u = ga * 4 + tt * 2 + ch
                            if u >= 2:
                                t.wait_ge(sem["aevv"], u - 1)
                            for cc in range(CC):
                                mm = t.matmul(PAV[u % 2][:, 0:512],
                                              xT[:, (ga % 2) * CC * 256 + cc * 256 + tt * 128:
                                                  (ga % 2) * CC * 256 + cc * 256 + tt * 128 + 128],
                                              wqkv_s[:, cc * 3 * D + 2 * D + ch * 512:
                                                  cc * 3 * D + 2 * D + ch * 512 + 512],
                                              start=(cc == 0), stop=(cc == CC - 1))
                            mm.then_inc(sem["apev"], 1)

            @blk.scalar
            def _(s):
                for ga in range(GA):
                    for m in range(16):
                        k = ga * 16 + m
                        s.wait_ge(sem["apeq"], k + 1)
                        if k >= 2:
                            s.wait_ge(P2("asp", k), 16 * (k // 2))
                        s.activation(qks[:, (k % 2) * 512:(k % 2) * 512 + 256],
                                     PAQ[k % 2][:, 0:256], AF.Copy,
                                     scale=(1.0 / 32.0) if m < 8 else 1.0
                                     ).then_inc(sem["aevq"], 1)

            @blk.vector
            def _(v):
                for ga in range(GA):
                    for tt in range(2):
                        for cc in range(CC):
                            j = ga * 16 + tt * 8 + cc
                            v.wait_ge(sem["apet"], P0_PET + j + 1)
                            if j == ga * 16 and ga >= 2:
                                v.wait_ge(sem["apeq"], (ga - 1) * 16)
                                v.wait_ge(sem["apev"], (ga - 1) * 4)
                            v.tensor_copy(xT[:, (ga % 2) * CC * 256 + cc * 256 + tt * 128:
                                              (ga % 2) * CC * 256 + cc * 256 + tt * 128 + 128],
                                          PAT[j % 2][:, 0:128]).then_inc(sem["aevt"], 1)
                    for tt in range(2):
                        for ch in range(2):
                            u = ga * 4 + tt * 2 + ch
                            v.wait_ge(sem["apev"], u + 1)
                            v.tensor_copy(v_sb[:, (ga * 2 + tt) * D + ch * 512:
                                               (ga * 2 + tt) * D + ch * 512 + 512],
                                          PAV[u % 2][:, 0:512]).then_inc(sem["aevv"], 1)

        A_SP = GA * 16 * 16
        A_EVV = GA * 4
        A_EVQ_BASE = 2 + GA * 16

        with ExitStack() as psB:
            ec = psB.enter_context
            qTg = ec(nc.sbuf_tensor("qTg", [128, 2 * CC * 256], BF16))
            ktc = ec(nc.sbuf_tensor("ktc", [128, 2 * CC * 512], BF16))
            ET = ec(nc.sbuf_tensor("ET", [128, 4 * 256], BF16))
            yst = ec(nc.sbuf_tensor("yst", [128, 2 * D], BF16))
            pbs = ec(nc.psum_tensor("pbs", [128, 512], F32))
            pbs2 = ec(nc.psum_tensor("pbs2", [128, 512], F32))
            pby0 = ec(nc.psum_tensor("pby0", [128, 1024], F32))
            pby1 = ec(nc.psum_tensor("pby1", [128, 1024], F32))
            pbr = ec(nc.psum_tensor("pbr", [128, 512], F32))
            pbr2 = ec(nc.psum_tensor("pbr2", [128, 512], F32))
            blk = ec(nc.Block())
            PBS = [pbs, pbs2]
            PBY = [pby0, pby1]
            PBR = [pbr, pbr2]

            @blk.sync
            def _(q):
                q.wait_ge(sem["asp"], A_SP // 2)
                q.wait_ge(sem["asp1"], A_SP // 2)
                q.wait_ge(sem["aevv"], A_EVV)
                q.wait_ge(sem["aevq"], 2 + GA * 16)
                q.wait_ge(sem["apeq"], GA * 16)
                q.wait_ge(sem["apev"], GA * 4)
                for g in range(GB):
                    if g >= 2:
                        q.wait_ge(sem["bpes"], NKC * (g - 1))
                    q.dma_start(out=qTg[:, (g % 2) * CC * 256:(g % 2 + 1) * CC * 256]
                                .rearrange("p (cc k) -> p cc k", cc=CC),
                                in_=_fm(qT_h, g * 256, 256)).then_inc(P2("bq", g), 16)
                    for jb in range(NCH):
                        jc = g * NCH + jb
                        if jc >= 2:
                            q.wait_ge(sem["bpes"], 4 * (jc - 1))
                        q.dma_start(out=ktc[:, (jc % 2) * CC * 512:(jc % 2 + 1) * CC * 512]
                                    .rearrange("p (cc k) -> p cc k", cc=CC),
                                    in_=_fm(kT_h, jb * 512, 512)).then_inc(P2("bk", jc), 16)

            @blk.tensor
            def _(t):
                t.wait_ge(sem["aevv"], A_EVV)
                t.wait_ge(sem["aevq"], 2)

                def y_step(p):
                    g_, kc_ = divmod(p, NKC)
                    t.wait_ge(sem["bexp"], p + 1)
                    if kc_ == 0:
                        t.wait_ge(sem["bdvey"], 3 * g_)
                        t.wait_ge(sem["brs0"], g_ + 1)
                    for qt in range(2):
                        lhs = ET[:, (p % 4) * 256 + qt * 128:(p % 4) * 256 + qt * 128 + 128]
                        t.matmul(PBY[qt][:, 0:512], lhs,
                                 v_sb[:, kc_ * D: kc_ * D + 512],
                                 start=(kc_ == 0), stop=(kc_ == NKC - 1))
                        t.matmul(PBY[qt][:, 512:1024], lhs,
                                 v_sb[:, kc_ * D + 512: kc_ * D + 1024],
                                 start=(kc_ == 0), stop=(kc_ == NKC - 1))
                        t.matmul(PBR[g_ % 2][:, qt:qt + 1], lhs, ones_b[:],
                                 start=False, stop=(kc_ == NKC - 1),
                                 skip_group_check=True).then_inc(sem["bpey"], 1)

                for g in range(GB):
                    t.wait_ge(P2("bq", g), 16 * (g // 2 + 1))
                    for kc in range(NKC):
                        gkc = g * NKC + kc
                        t.wait_ge(P2("bk", g * NCH + kc // 4), 16 * ((g * NCH + kc // 4) // 2 + 1))
                        if gkc >= 2:
                            t.wait_ge(sem["bexp"], gkc - 1)
                        for cc in range(CC):
                            mm = t.matmul(
                                PBS[gkc % 2][:, 0:256],
                                ktc[:, ((g * NCH + kc // 4) % 2) * CC * 512 + cc * 512
                                    + (kc % 4) * 128:
                                    ((g * NCH + kc // 4) % 2) * CC * 512 + cc * 512
                                    + (kc % 4) * 128 + 128],
                                qTg[:, (g % 2) * CC * 256 + cc * 256:
                                    (g % 2) * CC * 256 + cc * 256 + 256],
                                start=(cc == 0), stop=(cc == CC - 1))
                        mm.then_inc(sem["bpes"], 1)
                        if gkc > 0:
                            y_step(gkc - 1)
                y_step(GB * NKC - 1)

            @blk.scalar
            def _(s):
                s.wait_ge(sem["consts"], CONSTS)
                s.memzero(PBR[0][:, 0:2]).then_inc(sem["brs0"], 1)
                s.memzero(PBR[1][:, 0:2]).then_inc(sem["brs0"], 1)

                def evac_block(g_):
                    s.wait_ge(sem["bpey"], 2 * NKC * (g_ + 1))
                    s.activation(rs_all[:, 2 * g_: 2 * g_ + 2],
                                 PBR[g_ % 2][:, 0:2], AF.Copy).then_inc(sem["bdvey"], 1)
                    for qt in range(2):
                        s_ = 2 * g_ + qt
                        if s_ >= 2:
                            s.wait_ge(P2("bsp", s_), 16 * (s_ // 2))
                        s.activation(yst[:, (s_ % 2) * D:(s_ % 2) * D + D],
                                     PBY[qt][:, 0:1024], AF.Copy).then_inc(sem["bdvey"], 1)
                    if g_ + 2 < GB:
                        s.wait_ge(sem["bdvey"], 3 * g_ + 1)
                        s.memzero(PBR[g_ % 2][:, 0:2]).then_inc(sem["brs0"], 1)

                for g in range(GB):
                    for kc in range(NKC):
                        gkc = g * NKC + kc
                        s.wait_ge(sem["bpes"], gkc + 1)
                        if gkc >= 4:
                            s.wait_ge(sem["bpey"], 2 * (gkc - 3))
                        s.activation(ET[:, (gkc % 4) * 256:(gkc % 4) * 256 + 256],
                                     PBS[gkc % 2][:, 0:256], AF.Exp,
                                     bias=cns[:, 0:1]).then_inc(sem["bexp"], 1)
                        if kc == 0 and g >= 1:
                            evac_block(g - 1)
                evac_block(GB - 1)



            @blk.gpsimd
            def _(g_):
                for g in range(GB):
                    for qt in range(2):
                        s_ = 2 * g + qt
                        g_.wait_ge(sem["bdvey"], 3 * g + qt + 2)
                        g_.dma_start(out=y_h[s_ * 128:(s_ + 1) * 128, :],
                                     in_=yst[:, (s_ % 2) * D:(s_ % 2) * D + D]
                                     ).then_inc(P2("bsp", s_), 16)

        B_SP = 16 * 2 * GB
        esAB.close()

        esCD = es.enter_context(ExitStack())
        wfcq = esCD.enter_context(nc.sbuf_tensor("wfcq", [128, 2 * CC * (H // 4)], BF16))
        wpjq = esCD.enter_context(nc.sbuf_tensor("wpjq", [128, (H // 4 // 128) * D], BF16))

        with ExitStack() as psC:
            ec = psC.enter_context
            WoT = ec(nc.sbuf_tensor("WoT", [128, CC * D], BF16))
            wo_nat = ec(nc.sbuf_tensor("wo_nat", [128, CC * D], BF16))
            ybin = ec(nc.sbuf_tensor("ybin", [128, 2 * D], BF16))
            xres = ec(nc.sbuf_tensor("xres", [128, 2 * D], F32))
            yT = ec(nc.sbuf_tensor("yT", [128, 2 * CC * 128], BF16))
            v2 = ec(nc.sbuf_tensor("v2", [128, 2 * D], F32))
            x1f = ec(nc.sbuf_tensor("x1f", [128, 2 * D], F32))
            x1b = ec(nc.sbuf_tensor("x1b", [128, 2 * D], BF16))
            x1Ts = ec(nc.sbuf_tensor("x1Ts", [128, 2 * CC * 512], BF16))
            sqsc = ec(nc.sbuf_tensor("sqsc", [128, 2 * D], F32))
            lns = ec(nc.sbuf_tensor("lns", [128, 16], F32))
            r8 = ec(nc.sbuf_tensor("r8", [128, 2 * 8], F32))
            e8 = ec(nc.sbuf_tensor("e8", [128, 2 * 8], F32))
            rsc = ec(nc.sbuf_tensor("rsc", [128, 2 * 8], F32))
            rsc2 = ec(nc.sbuf_tensor("rsc2", [128, 2 * 8], F32))
            rsm = ec(nc.sbuf_tensor("rsm", [128, 8], F32))
            cTb = ec(nc.sbuf_tensor("cTb", [8, 2 * 128], BF16))
            inis = ec(nc.sbuf_tensor("inis", [128, 2 * D], F32))
            comb = ec(nc.sbuf_tensor("comb", [128, NT * 8], F32))
            Mall = ec(nc.sbuf_tensor("Mall", [128, NT * 8], F32))
            MT8 = ec(nc.sbuf_tensor("MT8", [8, 2 * 128], F32))
            PS8 = ec(nc.sbuf_tensor("PS8", [8, 2 * 128], F32))
            EX8 = ec(nc.sbuf_tensor("EX8", [8, 2 * 128], F32))
            MW8 = ec(nc.sbuf_tensor("MW8", [8, 2 * 128], F32))
            z8 = ec(nc.sbuf_tensor("z8", [8, 128], F32))
            rb8 = ec(nc.sbuf_tensor("rb8", [8, 2], F32))
            ecap = ec(nc.sbuf_tensor("ecap", [8, 2], F32))
            stk = ec(nc.sbuf_tensor("stk", [128, 2 * 8], F32))
            msk = ec(nc.sbuf_tensor("msk", [128, 2 * 8], F32))
            msk2 = ec(nc.sbuf_tensor("msk2", [128, 2 * 8], F32))
            sv = ec(nc.sbuf_tensor("sv", [128, 2 * 8], F32))
            p1s = ec(nc.sbuf_tensor("p1s", [128, 2 * 2], F32))
            p2s = ec(nc.sbuf_tensor("p2s", [128, 2 * 2], F32))
            si32 = ec(nc.sbuf_tensor("si32", [128, 2 * 2], mybir.dt.int32))
            io128 = ec(nc.sbuf_tensor("io128", [128, 1], mybir.dt.int32))
            io128f = ec(nc.sbuf_tensor("io128f", [128, 1], F32))
            io8 = ec(nc.sbuf_tensor("io8", [8, 1], mybir.dt.int32))
            zpk = ec(nc.sbuf_tensor("zpk", [128, (8 * (seq // 4 + 512)) * 2 // 128], F32))
            zrow = ec(nc.sbuf_tensor("zrow", [128, D], BF16))
            pct = ec(nc.psum_tensor("pct", [128, 1024], BF16))
            pct2 = ec(nc.psum_tensor("pct2", [128, 1024], BF16))
            pcao = ec(nc.psum_tensor("pcao", [128, 1024], F32))
            pcao2 = ec(nc.psum_tensor("pcao2", [128, 1024], F32))
            pcr = ec(nc.psum_tensor("pcr", [128, 512], F32))
            pccb = ec(nc.psum_tensor("pccb", [128, 512], F32))
            blk = nc.Block()
            blk.__enter__()
            PCT = [pct, pct2]
            PAO = [pcao, pcao2]

            CV0 = 64

            CVM = {}
            ev_cv = []

            @blk.vector
            def _(v):
                nv = 0
                prev = {}

                def emit(ti, name, waits, buildfn, is_ev=False):
                    nonlocal nv
                    for w in waits:
                        w()
                    if ti in prev:
                        v.wait_ge(sem["cv"], prev[ti])
                    buildfn().then_inc(sem["cv"], 1)
                    nv += 1
                    prev[ti] = nv
                    CVM[(ti, name)] = nv
                    if is_ev:
                        ev_cv.append(nv)

                for cc in range(CC):
                    for dd in range(CC):
                        jw = cc * CC + dd
                        v.wait_ge(sem["cpet"], jw + 1)
                        v.tensor_copy(WoT[:, dd * D + cc * 128: dd * D + cc * 128 + 128],
                                      PCT[jw % 2][:, 0:128]).then_inc(sem["cv"], 1)
                        nv += 1
                v.wait_ge(sem["rio"], 4)
                zpk3 = zpk[:].rearrange("p (f c) -> p f c", c=2)
                emit(-1, "zpk1", [], lambda: v.tensor_scalar(
                    zpk3[:, :, 0:1], zpk3[:, :, 0:1], float(seq), None, ALU.add))
                emit(-2, "z8", [], lambda: v.memset(z8[:], 0.0))
                emit(-2, "rb", [], lambda: v.memset(rb8[0:8, 0:1], 0.0))
                emit(-2, "iof", [], lambda: v.tensor_copy(io128f[:], io128[:]))
                emit(-2, "ef", [], lambda: v.tensor_copy(MT8[0:8, 0:1], io8[0:8, 0:1]))
                emit(-2, "ecap0", [], lambda: v.tensor_scalar(
                    ecap[0:8, 0:1], MT8[0:8, 0:1], float(CAP), None, ALU.mult))
                emit(-2, "ecap1", [], lambda: v.tensor_scalar(
                    ecap[0:8, 1:2], ecap[0:8, 0:1], float(CAP), None, ALU.add))
                CVM[("pre", "done")] = nv

                def prA(ti):
                    o2 = ti % 2
                    mt = MT8[0:8, o2 * 128: o2 * 128 + 128]
                    ps = PS8[0:8, o2 * 128: o2 * 128 + 128]
                    ex = EX8[0:8, o2 * 128: o2 * 128 + 128]
                    mw = MW8[0:8, o2 * 128: o2 * 128 + 128]
                    cmb = comb[:, ti * 8: ti * 8 + 8]
                    emit(ti, "MT", [lambda: v.wait_ge(sem["rpe"], 2 * ti + 1)],
                         lambda: v.tensor_copy(mt, PAO[ti % 2][0:8, 0:128]))
                    emit(ti, "scan", [], lambda: v.tensor_tensor_scan(
                        ps, mt, z8[0:8, :], rb8[0:8, ti % 2: ti % 2 + 1],
                        ALU.add, ALU.add))
                    emit(ti, "rbn", [], lambda: v.tensor_copy(
                        rb8[0:8, (ti + 1) % 2: (ti + 1) % 2 + 1], ps[0:8, 127:128]))
                    emit(ti, "ex", [], lambda: v.tensor_sub(ex, ps, mt))
                    emit(ti, "slotA", [], lambda: v.tensor_scalar(
                        ps, ex, ecap[0:8, 0:1], None, ALU.add))
                    emit(ti, "mwith", [], lambda: v.tensor_scalar(
                        mw, ps, ecap[0:8, 1:2], None, ALU.is_lt))
                    emit(ti, "M2", [], lambda: v.tensor_mul(mw, mw, mt))
                    emit(ti, "sf1", [], lambda: v.tensor_scalar(
                        ex, ps, -float(NSLOT + ti), None, ALU.add))
                    emit(ti, "sf2", [], lambda: v.tensor_mul(ex, ex, mw))
                    emit(ti, "sf3", [], lambda: v.tensor_scalar(
                        ex, ex, float(NSLOT + ti), None, ALU.add))
                    emit(ti, "stk", [lambda: v.wait_ge(sem["rpe"], 2 * ti + 2)],
                         lambda: v.tensor_copy(stk[:, o2 * 8: o2 * 8 + 8],
                                               (pcr if o2 == 0 else pccb)[0:128, 0:8]))
                    emit(ti, "s1", [], lambda: v.tensor_reduce(
                        sv[:, o2 * 8 + 0: o2 * 8 + 1], stk[:, o2 * 8: o2 * 8 + 8],
                        axis=AX, op=ALU.min))
                    emit(ti, "m1", [], lambda: v.tensor_scalar(
                        msk[:, o2 * 8: o2 * 8 + 8], stk[:, o2 * 8: o2 * 8 + 8],
                        sv[:, o2 * 8 + 0: o2 * 8 + 1], None, ALU.is_equal))
                    emit(ti, "cm1", [], lambda: v.tensor_mul(
                        msk2[:, o2 * 8: o2 * 8 + 8], cmb, msk[:, o2 * 8: o2 * 8 + 8]))
                    emit(ti, "c1", [], lambda: v.reduce_sum(
                        sv[:, o2 * 8 + 2: o2 * 8 + 3], msk2[:, o2 * 8: o2 * 8 + 8],
                        axis=AX))
                    emit(ti, "stm", [], lambda: v.scalar_tensor_tensor(
                        stk[:, o2 * 8: o2 * 8 + 8], msk[:, o2 * 8: o2 * 8 + 8], BIGS,
                        stk[:, o2 * 8: o2 * 8 + 8], ALU.mult, ALU.add))
                    emit(ti, "s2", [], lambda: v.tensor_reduce(
                        sv[:, o2 * 8 + 1: o2 * 8 + 2], stk[:, o2 * 8: o2 * 8 + 8],
                        axis=AX, op=ALU.min))
                    emit(ti, "m2", [], lambda: v.tensor_scalar(
                        msk[:, o2 * 8: o2 * 8 + 8], stk[:, o2 * 8: o2 * 8 + 8],
                        sv[:, o2 * 8 + 1: o2 * 8 + 2], None, ALU.is_equal))
                    emit(ti, "cm2", [], lambda: v.tensor_mul(
                        msk2[:, o2 * 8: o2 * 8 + 8], cmb, msk[:, o2 * 8: o2 * 8 + 8]))
                    emit(ti, "c2", [], lambda: v.reduce_sum(
                        sv[:, o2 * 8 + 3: o2 * 8 + 4], msk2[:, o2 * 8: o2 * 8 + 8],
                        axis=AX))
                    emit(ti, "tokf", [], lambda: v.tensor_scalar(
                        sv[:, o2 * 8 + 4: o2 * 8 + 5], io128f[:],
                        float(ti * 128), None, ALU.add))
                    pw_ = []
                    if ti >= 2:
                        pw_.append(lambda: v.wait_ge(P2("rst", ti), 32 * (ti // 2)))
                    emit(ti, "p1a", pw_, lambda: v.tensor_copy(
                        p1s[:, o2 * 2: o2 * 2 + 1], sv[:, o2 * 8 + 4: o2 * 8 + 5]))
                    emit(ti, "p1b", [], lambda: v.tensor_copy(
                        p1s[:, o2 * 2 + 1: o2 * 2 + 2], sv[:, o2 * 8 + 2: o2 * 8 + 3]))
                    emit(ti, "p2a", [], lambda: v.tensor_copy(
                        p2s[:, o2 * 2: o2 * 2 + 1], sv[:, o2 * 8 + 4: o2 * 8 + 5]))
                    emit(ti, "p2b", [], lambda: v.tensor_copy(
                        p2s[:, o2 * 2 + 1: o2 * 2 + 2], sv[:, o2 * 8 + 3: o2 * 8 + 4]))
                    emit(ti, "si1", [], lambda: v.tensor_copy(
                        si32[:, o2 * 2: o2 * 2 + 1], sv[:, o2 * 8 + 0: o2 * 8 + 1]))
                    emit(ti, "si2", [], lambda: v.tensor_copy(
                        si32[:, o2 * 2 + 1: o2 * 2 + 2], sv[:, o2 * 8 + 1: o2 * 8 + 2]))

                def recip(ti):
                    o2 = ti % 2
                    emit(ti, "recip", [], lambda: v.reciprocal(
                        lns[:, o2 * 8 + 7: o2 * 8 + 8], rs_all[:, ti: ti + 1]))

                def yTev(ti):
                    o2 = ti % 2
                    for dd in range(CC):
                        q = len(ev_cv)
                        emit(ti, f"yT{dd}",
                             [lambda q=q: v.wait_ge(sem["cpet"], CV0 + q + 1)],
                             lambda dd=dd, q=q: v.tensor_copy(
                                 yT[:, o2 * CC * 128 + dd * 128:
                                    o2 * CC * 128 + dd * 128 + 128],
                                 PCT[q % 2][:, 0:128]), is_ev=True)

                def resx1(ti):
                    o2 = ti % 2
                    emit(ti, "res",
                         [lambda: v.wait_ge(sem["cpeao"], ti + 1),
                          lambda: v.wait_ge(P2("cxr", ti), 16 * (ti // 2 + 1))],
                         lambda: v.scalar_tensor_tensor(
                             v2[:, o2 * D: o2 * D + D], PAO[ti % 2][:, 0:1024],
                             lns[:, o2 * 8 + 7: o2 * 8 + 8],
                             xres[:, o2 * D: o2 * D + D], ALU.mult, ALU.add))
                    emit(ti, "obr", [], lambda: v.tensor_add(
                        v2[:, o2 * D: o2 * D + D], v2[:, o2 * D: o2 * D + D], obr[:]))
                    emit(ti, "sum", [], lambda: v.reduce_sum(
                        lns[:, o2 * 8 + 0: o2 * 8 + 1], v2[:, o2 * D: o2 * D + D],
                        axis=AX))
                    emit(ti, "sq", [], lambda: v.tensor_mul(
                        sqsc[:, o2 * D: o2 * D + D], v2[:, o2 * D: o2 * D + D],
                        v2[:, o2 * D: o2 * D + D]))
                    emit(ti, "ssum", [], lambda: v.reduce_sum(
                        lns[:, o2 * 8 + 1: o2 * 8 + 2], sqsc[:, o2 * D: o2 * D + D],
                        axis=AX))
                    emit(ti, "mean", [], lambda: v.tensor_scalar(
                        lns[:, o2 * 8 + 2: o2 * 8 + 3], lns[:, o2 * 8 + 0: o2 * 8 + 1],
                        1.0 / D, None, ALU.mult))
                    emit(ti, "mean2", [], lambda: v.tensor_mul(
                        lns[:, o2 * 8 + 3: o2 * 8 + 4], lns[:, o2 * 8 + 2: o2 * 8 + 3],
                        lns[:, o2 * 8 + 2: o2 * 8 + 3]))
                    emit(ti, "var", [], lambda: v.scalar_tensor_tensor(
                        lns[:, o2 * 8 + 4: o2 * 8 + 5], lns[:, o2 * 8 + 1: o2 * 8 + 2],
                        1.0 / D, lns[:, o2 * 8 + 3: o2 * 8 + 4],
                        ALU.mult, ALU.subtract))
                    emit(ti, "x1", [lambda: v.wait_ge(sem["crstd"], ti + 1)],
                         lambda: v.tensor_scalar(
                             x1f[:, o2 * D: o2 * D + D], v2[:, o2 * D: o2 * D + D],
                             lns[:, o2 * 8 + 2: o2 * 8 + 3],
                             lns[:, o2 * 8 + 5: o2 * 8 + 6],
                             ALU.subtract, ALU.mult))

                def x1Tev(ti):
                    for cc in range(CC):
                        q = len(ev_cv)
                        emit(ti, f"x1T{cc}",
                             [lambda q=q: v.wait_ge(sem["cpet"], CV0 + q + 1)],
                             lambda cc=cc, q=q: v.tensor_copy(
                                 x1Ts[:, ((ti // 4) % 2) * CC * 512 + cc * 512
                                      + (ti % 4) * 128:
                                      ((ti // 4) % 2) * CC * 512 + cc * 512
                                      + (ti % 4) * 128 + 128],
                                 PCT[q % 2][:, 0:128]), is_ev=True)

                def router(ti):
                    o2 = ti % 2
                    emit(ti, "rmax", [lambda: v.wait_ge(sem["cper"], ti + 1)],
                         lambda: v.reduce_max(rsm[:, o2 * 4 + 0: o2 * 4 + 1],
                                              pcr[:, 0:8], axis=AX))
                    emit(ti, "t8", [], lambda: v.tensor_scalar(
                        r8[:, o2 * 8: o2 * 8 + 8], pcr[:, 0:8],
                        rsm[:, o2 * 4 + 0: o2 * 4 + 1], None, ALU.subtract))
                    emit(ti, "mask1", [], lambda: v.tensor_scalar(
                        rsc[:, o2 * 8: o2 * 8 + 8], r8[:, o2 * 8: o2 * 8 + 8],
                        0.0, None, ALU.is_equal))
                    emit(ti, "inv1", [], lambda: v.tensor_scalar(
                        rsc2[:, o2 * 8: o2 * 8 + 8], rsc[:, o2 * 8: o2 * 8 + 8],
                        -1.0, 1.0, ALU.mult, ALU.add))
                    emit(ti, "pw", [lambda: v.wait_ge(sem["ce8"], ti + 1)],
                         lambda: v.tensor_mul(rsc2[:, o2 * 8: o2 * 8 + 8],
                                              e8[:, o2 * 8: o2 * 8 + 8],
                                              rsc2[:, o2 * 8: o2 * 8 + 8]))
                    emit(ti, "second", [], lambda: v.reduce_max(
                        rsm[:, o2 * 4 + 1: o2 * 4 + 2], rsc2[:, o2 * 8: o2 * 8 + 8],
                        axis=AX))
                    emit(ti, "mask2", [], lambda: v.tensor_scalar(
                        rsc2[:, o2 * 8: o2 * 8 + 8], rsc2[:, o2 * 8: o2 * 8 + 8],
                        rsm[:, o2 * 4 + 1: o2 * 4 + 2], None, ALU.is_equal))
                    emit(ti, "wsum", [], lambda: v.tensor_add(
                        Mall[:, ti * 8: ti * 8 + 8], rsc[:, o2 * 8: o2 * 8 + 8],
                        rsc2[:, o2 * 8: o2 * 8 + 8]))
                    emit(ti, "cu", [], lambda: v.tensor_mul(
                        rsc[:, o2 * 8: o2 * 8 + 8], e8[:, o2 * 8: o2 * 8 + 8],
                        Mall[:, ti * 8: ti * 8 + 8]))
                    emit(ti, "den", [], lambda: v.tensor_scalar(
                        rsm[:, o2 * 4 + 2: o2 * 4 + 3], rsm[:, o2 * 4 + 1: o2 * 4 + 2],
                        1.0, None, ALU.add))
                    emit(ti, "r2", [], lambda: v.reciprocal(
                        rsm[:, o2 * 4 + 3: o2 * 4 + 4], rsm[:, o2 * 4 + 2: o2 * 4 + 3]))
                    emit(ti, "comb", [], lambda: v.tensor_scalar(
                        comb[:, ti * 8: ti * 8 + 8], rsc[:, o2 * 8: o2 * 8 + 8],
                        rsm[:, o2 * 4 + 3: o2 * 4 + 4], None, ALU.mult))

                def combTe(ti):
                    o2 = ti % 2
                    cw = [lambda: v.wait_ge(sem["cpecb"], ti + 1)]
                    if ti >= 2:
                        cw.append(lambda: v.wait_ge(sem["cpeb"], 2 * (ti - 1)))
                    emit(ti, "combTe", cw, lambda: v.tensor_copy(
                        cTb[0:8, o2 * 128: o2 * 128 + 128], pccb[0:8, 0:128]))

                def initf(ti):
                    o2 = ti % 2
                    iw = [lambda: v.wait_ge(sem["cpeb"], 2 * (ti + 1))]
                    if ti >= 2:
                        iw.append(lambda: v.wait_ge(P2("cspi", ti), 16 * (ti // 2)))
                    emit(ti, "init", iw, lambda: v.tensor_add(
                        inis[:, o2 * D: o2 * D + D], x1f[:, o2 * D: o2 * D + D],
                        PAO[ti % 2][:, 0:1024]))

                for g in range(NT // 2):
                    a, b = 2 * g, 2 * g + 1
                    recip(a)
                    yTev(a)
                    recip(b)
                    yTev(b)
                    resx1(a)
                    resx1(b)
                    x1Tev(a)
                    x1Tev(b)
                    router(a)
                    router(b)
                    combTe(a)
                    combTe(b)
                    initf(a)
                    initf(b)
                    prA(a)
                    prA(b)
                CVM[("all", "done")] = nv

            CV_TOT = CVM[("all", "done")]
            CV_PRE = CVM[("pre", "done")]

            def TPy(ti):
                return CV0 + (ti // 2) * 32 + (8 if ti % 2 == 0 else 16)

            @blk.sync
            def _(q):
                q.wait_ge(sem["bsp"], 16 * GB)
                q.wait_ge(sem["bsp1"], 16 * GB)
                q.wait_ge(sem["bexp"], GB * NKC)
                q.wait_ge(sem["bdvey"], 3 * GB)
                q.wait_ge(sem["brs0"], GB)
                for ti in range(NT):
                    q.wait_ge(P2("bsp", ti), 16 * (ti // 2 + 1))
                    if ti >= 2:
                        q.wait_ge(sem["cpet"], TPy(ti - 2))
                    q.dma_start(out=ybin[:, (ti % 2) * D:(ti % 2) * D + D],
                                in_=y_h[ti * 128:(ti + 1) * 128, :]).then_inc(P2("cy", ti), 16)
                    if ti >= 2:
                        q.wait_ge(sem["cv"], CVM[(ti - 2, "res")])
                    q.dma_start(out=xres[:, (ti % 2) * D:(ti % 2) * D + D]
                                .rearrange("p (u c) -> p u c", u=1),
                                in_=_tm(x_d, ti * 128, 1)).then_inc(P2("cxr", ti), 16)

            @blk.tensor
            def _(t):
                t.wait_ge(sem["bdvey"], 3 * GB)
                t.wait_ge(sem["bexp"], GB * NKC)
                t.wait_ge(sem["cwo"], 16)
                for cc in range(CC):
                    for dd in range(CC):
                        jw = cc * CC + dd
                        if jw >= 2:
                            t.wait_ge(sem["cv"], jw - 1)
                        t.transpose(PCT[jw % 2][:, 0:128],
                                    wo_nat[:, cc * D + dd * 128: cc * D + dd * 128 + 128],
                                    idb[:]).then_inc(sem["cpet"], 1)
                qn = [0]

                def trans(src_ap):
                    q = qn[0]
                    t.wait_ge(sem["cv"], ev_cv[q - 2] if q >= 2 else CV0 + q - 1)
                    t.transpose(PCT[q % 2][:, 0:128], src_ap, idb[:]).then_inc(sem["cpet"], 1)
                    qn[0] += 1

                def yTt(ti):
                    t.wait_ge(P2("cy", ti), 16 * (ti // 2 + 1))
                    for dd in range(CC):
                        trans(ybin[:, (ti % 2) * D + dd * 128:(ti % 2) * D + dd * 128 + 128])

                def ao(ti):
                    t.wait_ge(sem["cv"], CVM[(ti, "yT7")])
                    if ti >= 2:
                        t.wait_ge(sem["cv"], CVM[(ti - 2, "MT")])
                    for ch in range(2):
                        for dd in range(CC):
                            mm = t.matmul(PAO[ti % 2][:, ch * 512: ch * 512 + 512],
                                          yT[:, (ti % 2) * CC * 128 + dd * 128:
                                              (ti % 2) * CC * 128 + dd * 128 + 128],
                                          WoT[:, dd * D + ch * 512: dd * D + ch * 512 + 512],
                                          start=(dd == 0), stop=(dd == CC - 1))
                    mm.then_inc(sem["cpeao"], 1)

                def x1Tt(ti):
                    t.wait_ge(sem["cx1bf"], ti + 1)
                    for cc in range(CC):
                        trans(x1b[:, (ti % 2) * D + cc * 128:(ti % 2) * D + cc * 128 + 128])

                def router(ti):
                    t.wait_ge(sem["cv"], CVM[(ti, "x1T7")])
                    if ti >= 1:
                        t.wait_ge(sem["cv"], CVM[(ti - 1, "t8")])
                    pe_ = 2 * (ti // 2) - 2
                    if pe_ >= 0:
                        t.wait_ge(sem["cv"], CVM[(pe_, "stk")])
                    for cc in range(CC):
                        mm = t.matmul(pcr[:, 0:8],
                                      x1Ts[:, ((ti // 4) % 2) * CC * 512 + cc * 512
                                          + (ti % 4) * 128:
                                          ((ti // 4) % 2) * CC * 512 + cc * 512
                                          + (ti % 4) * 128 + 128],
                                      RwT[:, cc * 8: cc * 8 + 8],
                                      start=(cc == 0), stop=(cc == CC - 1))
                    mm.then_inc(sem["cper"], 1)

                def combt(ti):
                    t.wait_ge(sem["cv"], CVM[(ti, "comb")])
                    if ti >= 1:
                        t.wait_ge(sem["cv"], CVM[(ti - 1, "combTe")])
                    po_ = ti - 1 if ti % 2 == 0 else ti - 2
                    if po_ >= 0:
                        t.wait_ge(sem["cv"], CVM[(po_, "stk")])
                    t.transpose(pccb[0:8, 0:128], comb[:, ti * 8: ti * 8 + 8],
                                idf[:]).then_inc(sem["cpecb"], 1)

                def bm(ti):
                    t.wait_ge(sem["cv"], CVM[(ti, "combTe")])
                    for ch in range(2):
                        t.matmul(PAO[ti % 2][:, ch * 512: ch * 512 + 512],
                                 cTb[0:8, (ti % 2) * 128:(ti % 2) * 128 + 128],
                                 bpj_s[0:8, ch * 512: ch * 512 + 512],
                                 start=True, stop=True).then_inc(sem["cpeb"], 1)

                def prT(ti):
                    t.wait_ge(sem["cv"], CVM[(ti, "wsum")])
                    t.wait_ge(sem["cv"], CVM[(ti, "init")])
                    t.transpose(PAO[ti % 2][0:8, 0:128], Mall[:, ti * 8: ti * 8 + 8],
                                idf[:]).then_inc(sem["rpe"], 1)
                    t.wait_ge(sem["cv"], CVM[(ti, "sf3")])
                    if ti % 2 == 0:
                        t.wait_ge(sem["cv"], CVM[(ti + 1, "t8")])
                    else:
                        t.wait_ge(sem["cv"], CVM[(ti, "combTe")])
                    t.transpose((pcr if ti % 2 == 0 else pccb)[0:128, 0:8],
                                EX8[0:8, (ti % 2) * 128: (ti % 2) * 128 + 128],
                                idf[0:8, 0:8]).then_inc(sem["rpe"], 1)

                for g in range(NT // 2):
                    a, b = 2 * g, 2 * g + 1
                    yTt(a)
                    yTt(b)
                    ao(a)
                    ao(b)
                    x1Tt(a)
                    x1Tt(b)
                    router(a)
                    router(b)
                    combt(a)
                    combt(b)
                    bm(a)
                    bm(b)
                    prT(a)
                    prT(b)

            @blk.scalar
            def _(s):
                s.wait_ge(sem["consts"], CONSTS)
                na = [0]

                def rstd_cast(ti):
                    s.wait_ge(sem["cv"], CVM[(ti, "var")])
                    s.activation(lns[:, (ti % 2) * 8 + 6:(ti % 2) * 8 + 7],
                                 lns[:, (ti % 2) * 8 + 4:(ti % 2) * 8 + 5],
                                 AF.Ln, bias=cns[:, 1:2]).then_inc(sem["ca"], 1)
                    na[0] += 1
                    s.wait_ge(sem["ca"], na[0])
                    s.activation(lns[:, (ti % 2) * 8 + 5:(ti % 2) * 8 + 6],
                                 lns[:, (ti % 2) * 8 + 6:(ti % 2) * 8 + 7],
                                 AF.Exp, scale=-0.5).then_inc(sem["crstd"], 1)
                    s.wait_ge(sem["cv"], CVM[(ti, "x1")])
                    if ti >= 2:
                        s.wait_ge(P2("cspx", ti), 16 * (ti // 2))
                    s.activation(x1b[:, (ti % 2) * D:(ti % 2) * D + D],
                                 x1f[:, (ti % 2) * D:(ti % 2) * D + D],
                                 AF.Copy).then_inc(sem["cx1bf"], 1)

                def e8f(ti):
                    s.wait_ge(sem["cv"], CVM[(ti, "t8")])
                    s.activation(e8[:, (ti % 2) * 8:(ti % 2) * 8 + 8],
                                 r8[:, (ti % 2) * 8:(ti % 2) * 8 + 8],
                                 AF.Exp).then_inc(sem["ce8"], 1)

                for g in range(NT // 2):
                    a, b = 2 * g, 2 * g + 1
                    rstd_cast(a)
                    rstd_cast(b)
                    e8f(a)
                    e8f(b)

            RPK_TOT = 16 * 2
            HQ_ = H // 4
            HT_ = HQ_ // 128

            @blk.gpsimd
            def _(g_):
                g_.memset(zrow[:], 0.0).then_inc(sem["rio"], 1)
                g_.memset(zpk[:], 0.0).then_inc(sem["rio"], 1)
                g_.iota(io128[:], [[0, 1]], channel_multiplier=1,
                        allow_small_or_imprecise_dtypes=True).then_inc(sem["rio"], 1)
                g_.iota(io8[0:8, 0:1], [[0, 1]], channel_multiplier=1,
                        allow_small_or_imprecise_dtypes=True).then_inc(sem["rio"], 1)
                g_.wait_ge(sem["bsp"], 16 * GB)
                g_.wait_ge(sem["bsp1"], 16 * GB)
                g_.wait_ge(sem["bdvey"], 3 * GB)
                g_.dma_start(out=wo_nat[:].rearrange("p (cc d) -> p cc d", cc=CC),
                             in_=_fm(wo_d, 0, D)).then_inc(sem["cwo"], 16)
                g_.dma_start(out=wfcq[:, 0:CC * HQ_]
                             .rearrange("p (cc h) -> p cc h", cc=CC),
                             in_=wfc_d[0:D, 0:HQ_]
                             .rearrange("(cc p) h -> p cc h", p=128)
                             ).then_inc(sem["dw"], 16)
                g_.dma_start(out=wpjq[:].rearrange("p (ht c) -> p ht c", ht=HT_),
                             in_=wpj_d[0:HQ_, :]
                             .rearrange("(ht p) c -> p ht c", p=128)
                             ).then_inc(sem["dpk"], 16)
                g_.dma_start(out=x1b_h[seq:seq + 128, :],
                             in_=zrow[:]).then_inc(sem["rpk"], 16)
                g_.wait_ge(sem["cv"], CV_PRE)
                g_.dma_start(out=pck_h[0:NSLOT, :].rearrange("(p f) c -> p (f c)", p=128),
                             in_=zpk[:]).then_inc(sem["rpk"], 16)
                for ti in range(NT):
                    o2 = ti % 2
                    g_.wait_ge(sem["cx1bf"], ti + 1)
                    g_.dma_start(out=x1b_h[ti * 128:(ti + 1) * 128, :],
                                 in_=x1b[:, o2 * D: o2 * D + D]
                                 ).then_inc(P2("cspx", ti), 16)
                    g_.wait_ge(sem["cv"], CVM[(ti, "init")])
                    g_.dma_start(out=yin_h[ti * 128:(ti + 1) * 128, :],
                                 in_=inis[:, o2 * D: o2 * D + D]
                                 ).then_inc(P2("cspi", ti), 16)
                    g_.wait_ge(sem["cv"], CVM[(ti, "si2")])
                    g_.wait_ge(sem["rpk"], 16)
                    g_.indirect_dma_start(
                        out=pck_h, out_offset=bass.IndirectOffsetOnAxis(
                            si32[:, o2 * 2: o2 * 2 + 1], 0),
                        in_=p1s[:, o2 * 2: o2 * 2 + 2], in_offset=None,
                        ).then_inc(P2("rst", ti), 16)
                    g_.indirect_dma_start(
                        out=pck_h, out_offset=bass.IndirectOffsetOnAxis(
                            si32[:, o2 * 2 + 1: o2 * 2 + 2], 0),
                        in_=p2s[:, o2 * 2: o2 * 2 + 2], in_offset=None,
                        ).then_inc(P2("rst", ti), 16)
            blk.__exit__(None, None, None)

        C_SPX = 16 * (NT // 4)
        C_SPI = 16 * NT

        FC_PER_E = 4 * SGE * HT
        PJ_PER_E = 4 * SGE * 8 // 4
        EV_PER_E = 4 * SGE * 8
        with ExitStack() as psD:
            ec = psD.enter_context
            x1gT = ec(nc.sbuf_tensor("x1gT", [128, CC * CAP], BF16))
            acc = ec(nc.sbuf_tensor("acc", [128, STE * D], F32))
            hTs = ec(nc.sbuf_tensor("hT", [128, 2 * HT * 512], BF16))
            x1g = ec(nc.sbuf_tensor("x1g", [128, 6 * D], BF16))
            pcke = ec(nc.sbuf_tensor("pcke", [128, 2 * STE * 2], F32))
            idxe = ec(nc.sbuf_tensor("idxe", [128, E * STE], mybir.dt.int32))
            csle = ec(nc.sbuf_tensor("csle", [128, E * STE], F32))
            pdh = ec(nc.psum_tensor("pdh", [128, 512], F32))
            pdh2 = ec(nc.psum_tensor("pdh2", [128, 512], F32))
            pdp = ec(nc.psum_tensor("pdp", [128, 512], F32))
            pdp2 = ec(nc.psum_tensor("pdp2", [128, 512], F32))
            pdt = ec(nc.psum_tensor("pdt", [128, 1024], BF16))
            pdt2 = ec(nc.psum_tensor("pdt2", [128, 1024], BF16))
            blk = ec(nc.Block())
            PDH = [pdh, pdh2]
            PDP = [pdp, pdp2]
            PDT = [pdt, pdt2]

            sched = [("f", 0), ("f", 1)]
            for i in range(SGE - 2):
                sched += [("p", i), ("f", i + 2)]
            sched += [("p", SGE - 2), ("p", SGE - 1)]

            @blk.gpsimd
            def _(g_):
                g_.wait_ge(sem["cv"], CV_TOT)
                g_.wait_ge(sem["rpk"], RPK_TOT)
                g_.wait_ge(sem["rst"], 32 * (NT // 2))
                g_.wait_ge(sem["rst1"], 32 * (NT // 2))
                g_.wait_ge(sem["cspx"], 16 * (NT // 2))
                g_.wait_ge(sem["cspx1"], 16 * (NT // 2))
                g_.wait_ge(sem["cspi"], 16 * (NT // 2))
                g_.wait_ge(sem["cspi1"], 16 * (NT // 2))

                def scatters(em):
                    for st in range(STE):
                        T3 = (em * 4 + 3) * SGE + st // 4
                        g_.wait_ge(sem["ddve"], T3 * 8 + (st % 4) * 2 + 2)
                        g_.indirect_dma_start(
                            out=yin_h, out_offset=bass.IndirectOffsetOnAxis(
                                idxe[:, em * STE + st: em * STE + st + 1], 0),
                            in_=acc[:, st * D: st * D + D], in_offset=None,
                            compute_op=ALU.add).then_inc(sem["dsca"], 16)

                def wpj_load(e_, hq_):
                    idx_ = e_ * 4 + hq_
                    g_.wait_ge(sem["dpep"], 8 * SGE * idx_)
                    g_.dma_start(out=wpjq[:].rearrange("p (ht c) -> p ht c", ht=HT),
                                 in_=wpj_d[e_ * H + hq_ * HQ: e_ * H + (hq_ + 1) * HQ, :]
                                 .rearrange("(ht p) c -> p ht c", p=128)
                                 ).then_inc(sem["dpk"], 16)

                for e in range(E):
                    idx = e * 4
                    if e >= 1:
                        g_.wait_ge(sem["dpep"], 8 * SGE * (idx - 1))
                        g_.dma_start(out=wfcq[:, (idx % 2) * CC * HQ:(idx % 2 + 1) * CC * HQ]
                                     .rearrange("p (cc h) -> p cc h", cc=CC),
                                     in_=wfc_d[e * D:(e + 1) * D, 0:HQ]
                                     .rearrange("(cc p) h -> p cc h", p=128)
                                     ).then_inc(P2("dw", idx), 16)
                    if e >= 1:
                        scatters(e - 1)
                        wpj_load(e, 0)
                    for st in range(STE):
                        gst = e * STE + st
                        g_.wait_ge(sem["dic"], 2 * STE * e + 2 * st + 1)
                        if gst >= 6:
                            g_.wait_ge(P2("dg", gst), 16 * ((gst // 2) - 2))
                            g_.wait_ge(sem["dgt"], 8 * (gst - 5))
                        g_.indirect_dma_start(
                            out=x1g[:, (gst % 6) * D:(gst % 6) * D + D],
                            out_offset=None,
                            in_=x1b_h,
                            in_offset=bass.IndirectOffsetOnAxis(
                                idxe[:, e * STE + st: e * STE + st + 1], 0),
                            ).then_inc(P2("dg", gst), 16)
                    for hq in range(1, 4):
                        idx = e * 4 + hq
                        g_.wait_ge(sem["dpep"], 8 * SGE * (idx - 1))
                        g_.dma_start(out=wfcq[:, (idx % 2) * CC * HQ:(idx % 2 + 1) * CC * HQ]
                                     .rearrange("p (cc h) -> p cc h", cc=CC),
                                     in_=wfc_d[e * D:(e + 1) * D, hq * HQ:(hq + 1) * HQ]
                                     .rearrange("(cc p) h -> p cc h", p=128)
                                     ).then_inc(P2("dw", idx), 16)
                        wpj_load(e, hq)
                scatters(E - 1)

            @blk.tensor
            def _(t):
                for e in range(E):
                    for st in range(STE):
                        gst = e * STE + st
                        t.wait_ge(P2("dg", gst), 16 * (gst // 2 + 1))
                        for cc in range(CC):
                            jj = gst * 8 + cc
                            if jj >= 2:
                                t.wait_ge(sem["dgv"], jj - 1)
                            t.transpose(PDT[jj % 2][:, 0:128],
                                        x1g[:, (gst % 6) * D + cc * 128:
                                            (gst % 6) * D + cc * 128 + 128],
                                        idb[:]).then_inc(sem["dgt"], 1)
                    for hq in range(4):
                        idx = e * 4 + hq

                        def fc(sg):
                            if hq == 0:
                                t.wait_ge(sem["dgv"], 8 * STE * e + 32 * (sg + 1))
                            T = idx * SGE + sg
                            for ht in range(HT):
                                j = T * HT + ht
                                if j >= 2:
                                    t.wait_ge(sem["dact"], j - 1)
                                for cc in range(CC):
                                    mm = t.matmul(
                                        PDH[j % 2][:, 0:512],
                                        wfcq[:, (idx % 2) * CC * HQ + cc * HQ + ht * 128:
                                             (idx % 2) * CC * HQ + cc * HQ + ht * 128 + 128],
                                        x1gT[:, cc * CAP + sg * 512: cc * CAP + sg * 512 + 512],
                                        start=(cc == 0), stop=(cc == CC - 1))
                                mm.then_inc(sem["dpeh"], 1)

                        def proj(sg):
                            T = idx * SGE + sg
                            t.wait_ge(sem["dact"], HT * (T + 1))
                            for tt in range(4):
                                for ch in range(2):
                                    p = T * 8 + tt * 2 + ch
                                    if p >= 2:
                                        t.wait_ge(sem["ddve"], p - 1)
                                    for ht in range(HT):
                                        mm = t.matmul(
                                            PDP[p % 2][:, 0:512],
                                            hTs[:, (T % 2) * HT * 512 + ht * 512 + tt * 128:
                                                (T % 2) * HT * 512 + ht * 512 + tt * 128 + 128],
                                            wpjq[:, ht * D + ch * 512: ht * D + ch * 512 + 512],
                                            start=(ht == 0), stop=(ht == HT - 1))
                                    mm.then_inc(sem["dpep"], 1)

                        t.wait_ge(P2("dw", idx), 16 * (idx // 2 + 1))
                        first_p = True
                        for kind, sg in sched:
                            if kind == "f":
                                fc(sg)
                            else:
                                if first_p:
                                    t.wait_ge(sem["dpk"], 16 * (idx + 1))
                                    first_p = False
                                proj(sg)

            @blk.scalar
            def _(s):
                s.wait_ge(sem["consts"], CONSTS)
                def pcke_load(e_):
                    s.dma_start(out=pcke[:, (e_ % 2) * STE * 2:(e_ % 2 + 1) * STE * 2]
                                .rearrange("p (t c) -> p t c", t=STE),
                                in_=pck_h[e_ * CAP: e_ * CAP + STE * 128, :]
                                .rearrange("(t p) c -> p t c", p=128)
                                ).then_inc(sem["dx1"], 16)

                for e in range(E):
                    if e == 0:
                        s.wait_ge(sem["rpk"], RPK_TOT)
                        s.wait_ge(sem["rst"], 32 * (NT // 2))
                        s.wait_ge(sem["rst1"], 32 * (NT // 2))
                        pcke_load(0)
                        pcke_load(1)
                    elif e + 1 < E:
                        s.wait_ge(sem["dic"], 2 * STE * e)
                        pcke_load(e + 1)
                    for hq in range(4):
                        idx = e * 4 + hq
                        for sg in range(SGE):
                            T = idx * SGE + sg
                            for ht in range(HT):
                                j = T * HT + ht
                                s.wait_ge(sem["dpeh"], j + 1)
                                if T >= 2:
                                    s.wait_ge(sem["dpep"], 8 * (T - 1))
                                s.activation(hTs[:, (T % 2) * HT * 512 + ht * 512:
                                                 (T % 2) * HT * 512 + ht * 512 + 512],
                                             PDH[j % 2][:, 0:512],
                                             AF.Gelu_apprx_tanh,
                                             bias=bfc_s[:, (hq * HT + ht) * 8 + e:
                                                        (hq * HT + ht) * 8 + e + 1]
                                             ).then_inc(sem["dact"], 1)

            @blk.vector
            def _(v):
                def casts(e):
                    v.wait_ge(sem["dx1"], 16 * (e + 1))
                    p0 = (e % 2) * STE * 2
                    for st in range(STE):
                        v.tensor_copy(idxe[:, e * STE + st: e * STE + st + 1],
                                      pcke[:, p0 + st * 2: p0 + st * 2 + 1]).then_inc(sem["dic"], 1)
                        v.tensor_copy(csle[:, e * STE + st: e * STE + st + 1],
                                      pcke[:, p0 + st * 2 + 1: p0 + st * 2 + 2]).then_inc(sem["dic"], 1)

                casts(0)
                for e in range(E):
                    for st in range(STE):
                        gst = e * STE + st
                        for cc in range(CC):
                            jj = gst * 8 + cc
                            v.wait_ge(sem["dgt"], jj + 1)
                            if st == 0 and cc == 0 and e >= 1:
                                v.wait_ge(sem["dpeh"], FC_PER_E * e)
                            v.tensor_copy(x1gT[:, cc * CAP + st * 128:
                                               cc * CAP + st * 128 + 128],
                                          PDT[jj % 2][:, 0:128]).then_inc(sem["dgv"], 1)
                    if e + 1 < E:
                        casts(e + 1)
                    for hq in range(4):
                        idx = e * 4 + hq
                        for sg in range(SGE):
                            T = idx * SGE + sg
                            for tt in range(4):
                                for ch in range(2):
                                    p = T * 8 + tt * 2 + ch
                                    v.wait_ge(sem["dpep"], p + 1)
                                    if hq == 0 and sg == 0 and tt == 0 and ch == 0:
                                        v.wait_ge(sem["dic"], 2 * STE * (e + 1))
                                        if e >= 1:
                                            v.wait_ge(sem["dsca"], 16 * STE * e)
                                    sl = (sg * 4 + tt) * D + ch * 512
                                    cs = csle[:, e * STE + sg * 4 + tt:
                                              e * STE + sg * 4 + tt + 1]
                                    if hq == 0:
                                        v.tensor_scalar(acc[:, sl: sl + 512],
                                                        PDP[p % 2][:, 0:512],
                                                        cs, None,
                                                        ALU.mult).then_inc(sem["ddve"], 1)
                                    else:
                                        v.scalar_tensor_tensor(
                                            acc[:, sl: sl + 512],
                                            PDP[p % 2][:, 0:512], cs,
                                            acc[:, sl: sl + 512],
                                            ALU.mult, ALU.add).then_inc(sem["ddve"], 1)

        with ExitStack() as psE:
            ec = psE.enter_context
            yinb = ec(nc.sbuf_tensor("yinb", [128, 2 * D], F32))
            outst = ec(nc.sbuf_tensor("outst", [128, 2 * D], F32))
            dlns = ec(nc.sbuf_tensor("dlns", [128, 8], F32))
            dsq = ec(nc.sbuf_tensor("dsq", [128, D], F32))
            blk = ec(nc.Block())

            @blk.sync
            def _(q):
                q.wait_ge(sem["dsca"], 16 * STE * E)
                for ti in range(NT):
                    if ti >= 2:
                        q.wait_ge(sem["dv"], 7 * (ti - 2) + 7)
                    q.dma_start(out=yinb[:, (ti % 2) * D:(ti % 2) * D + D],
                                in_=yin_h[ti * 128:(ti + 1) * 128, :]
                                ).then_inc(P2("cy", ti), 16)

            @blk.scalar
            def _(s):
                na = 0
                for ti in range(NT):
                    o2 = ti % 2
                    s.wait_ge(sem["dv"], 7 * ti + 6)
                    s.activation(dlns[:, o2 * 4 + 3: o2 * 4 + 4],
                                 dlns[:, o2 * 4 + 1: o2 * 4 + 2],
                                 AF.Ln, bias=cns[:, 1:2]).then_inc(sem["da"], 1)
                    na += 1
                    s.wait_ge(sem["da"], na)
                    s.activation(dlns[:, o2 * 4 + 3: o2 * 4 + 4],
                                 dlns[:, o2 * 4 + 3: o2 * 4 + 4],
                                 AF.Exp, scale=-0.5).then_inc(sem["drstd"], 1)

            @blk.vector
            def _(v):
                nv = 0

                def W():
                    v.wait_ge(sem["dv"], nv)

                def V(instr):
                    nonlocal nv
                    instr.then_inc(sem["dv"], 1)
                    nv += 1

                for ti in range(NT):
                    o2 = ti % 2
                    c0 = o2 * 4
                    at = yinb[:, o2 * D: o2 * D + D]
                    v.wait_ge(P2("cy", ti), 16 * (NT // 2 + ti // 2 + 1))
                    W()
                    V(v.reduce_sum(dlns[:, c0 + 0: c0 + 1], at, axis=AX))
                    W()
                    V(v.tensor_mul(dsq[:, 0:D], at, at))
                    W()
                    V(v.reduce_sum(dlns[:, c0 + 1: c0 + 2], dsq[:, 0:D], axis=AX))
                    W()
                    V(v.tensor_scalar(dlns[:, c0 + 2: c0 + 3],
                                      dlns[:, c0 + 0: c0 + 1],
                                      1.0 / D, None, ALU.mult))
                    W()
                    V(v.tensor_mul(dlns[:, c0 + 0: c0 + 1],
                                   dlns[:, c0 + 2: c0 + 3],
                                   dlns[:, c0 + 2: c0 + 3]))
                    W()
                    V(v.scalar_tensor_tensor(dlns[:, c0 + 1: c0 + 2],
                                             dlns[:, c0 + 1: c0 + 2],
                                             1.0 / D,
                                             dlns[:, c0 + 0: c0 + 1],
                                             ALU.mult, ALU.subtract))
                    v.wait_ge(sem["drstd"], ti + 1)
                    if ti >= 2:
                        v.wait_ge(P2("dout", ti), 16 * (ti // 2))
                    W()
                    V(v.tensor_scalar(outst[:, o2 * D: o2 * D + D], at,
                                      dlns[:, c0 + 2: c0 + 3],
                                      dlns[:, c0 + 3: c0 + 4],
                                      ALU.subtract, ALU.mult))

            @blk.gpsimd
            def _(g_):
                for ti in range(NT):
                    g_.wait_ge(sem["dv"], 7 * ti + 7)
                    g_.dma_start(out=out_d[ti * 128:(ti + 1) * 128, :],
                                 in_=outst[:, (ti % 2) * D:(ti % 2) * D + D]
                                 ).then_inc(P2("dout", ti), 16)
                g_.wait_ge(sem["dout"], 16 * (NT // 2))
                g_.wait_ge(sem["dout1"], 16 * (NT // 2))
    return nc


def kernel(L0_attn_x, L0_attn_weight_qkv, L0_attn_out_w, L0_attn_out_b,
           L0_ffn_router_w, L0_ffn_w_fc, L0_ffn_b_fc, L0_ffn_w_proj, L0_ffn_b_proj):
    x = np.ascontiguousarray(np.asarray(L0_attn_x, dtype=np.float32))
    B, seq, d = x.shape
    wqkv = np.ascontiguousarray(np.asarray(L0_attn_weight_qkv, dtype=np.float32))
    wo = np.ascontiguousarray(np.asarray(L0_attn_out_w, dtype=np.float32))
    ob = np.ascontiguousarray(np.asarray(L0_attn_out_b, dtype=np.float32)).reshape(1, d)
    rw = np.ascontiguousarray(np.asarray(L0_ffn_router_w, dtype=np.float32))
    wfc = np.ascontiguousarray(np.asarray(L0_ffn_w_fc, dtype=np.float32))
    bfc = np.ascontiguousarray(np.asarray(L0_ffn_b_fc, dtype=np.float32))
    wpj = np.ascontiguousarray(np.asarray(L0_ffn_w_proj, dtype=np.float32))
    bpj = np.ascontiguousarray(np.asarray(L0_ffn_b_proj, dtype=np.float32))
    Ex, _, H = wfc.shape
    wfc2 = wfc.reshape(Ex * d, H)
    wpj2 = wpj.reshape(Ex * H, d)
    idb = np.eye(128, dtype=ml_dtypes.bfloat16)
    idf = np.eye(128, dtype=np.float32)

    nc = build(seq=seq, H=H)
    in_maps = []
    for b in range(B):
        in_maps.append({
            "L0_attn_x": x[b],
            "L0_attn_weight_qkv": wqkv,
            "L0_attn_out_w": wo,
            "L0_attn_out_b": ob,
            "L0_ffn_router_w": rw,
            "L0_ffn_w_fc": wfc2,
            "L0_ffn_b_fc": bfc,
            "L0_ffn_w_proj": wpj2,
            "L0_ffn_b_proj": bpj,
            "ident_bf": idb,
            "ident_f32": idf,
        })
    res = run_bass_kernel_spmd(nc, in_maps, core_ids=list(range(B)))
    return np.stack([r["out"] for r in res.results], axis=0)


# version 18
# speedup vs baseline: 1.0470x; 1.0470x over previous
import numpy as np
import ml_dtypes
from contextlib import ExitStack

import concourse.bass as bass
import concourse.mybir as mybir
from concourse.bass_utils import run_bass_kernel_spmd

F32 = mybir.dt.float32
BF16 = mybir.dt.bfloat16
AF = mybir.ActivationFunctionType
ALU = mybir.AluOpType
AX = mybir.AxisListType.X

SHIFT = 12.0
LN_EPS = 1e-5
D = 1024
CC = D // 128
E = 8


def _fm(ap2d, c0, cn):
    return ap2d[:, c0:c0 + cn].rearrange("(cc p) k -> p cc k", p=128)


def _tm(ap2d, r0, nt):
    return ap2d[r0:r0 + nt * 128, :].rearrange("(t p) c -> p t c", p=128)


def build(seq=8192, H=4096):
    assert seq % 2048 == 0 and H % 512 == 0
    GA = seq // 256
    GB = seq // 256
    NKC = seq // 128
    NCH = seq // 512
    NT = seq // 128
    TQ = 2048
    NQ = seq // TQ
    HQ = H // 4
    HT = HQ // 128
    HTT = H // 128
    CAP = seq // 4 + 512
    NSLOT = E * CAP
    STE = CAP // 128
    SGE = CAP // 512
    BIGS = 1.0e8

    nc = bass.Bass("TRN2", target_bir_lowering=False)

    x_d = nc.dram_tensor("L0_attn_x", [seq, D], F32, kind="ExternalInput").ap()
    wqkv_d = nc.dram_tensor("L0_attn_weight_qkv", [D, 3 * D], F32, kind="ExternalInput").ap()
    wo_d = nc.dram_tensor("L0_attn_out_w", [D, D], F32, kind="ExternalInput").ap()
    ob_d = nc.dram_tensor("L0_attn_out_b", [1, D], F32, kind="ExternalInput").ap()
    rw_d = nc.dram_tensor("L0_ffn_router_w", [E, D], F32, kind="ExternalInput").ap()
    wfc_d = nc.dram_tensor("L0_ffn_w_fc", [E * D, H], F32, kind="ExternalInput").ap()
    bfc_d = nc.dram_tensor("L0_ffn_b_fc", [E, H], F32, kind="ExternalInput").ap()
    wpj_d = nc.dram_tensor("L0_ffn_w_proj", [E * H, D], F32, kind="ExternalInput").ap()
    pck_h = nc.dram_tensor("pck_h", [8 * (seq // 4 + 512) + seq // 128, 2], F32, kind="Internal").ap()
    bpj_d = nc.dram_tensor("L0_ffn_b_proj", [E, D], F32, kind="ExternalInput").ap()
    idb_d = nc.dram_tensor("ident_bf", [128, 128], BF16, kind="ExternalInput").ap()
    idf_d = nc.dram_tensor("ident_f32", [128, 128], F32, kind="ExternalInput").ap()

    out_d = nc.dram_tensor("out", [seq, D], F32, kind="ExternalOutput").ap()

    qT_h = nc.dram_tensor("qT_h", [D, seq], BF16, kind="Internal").ap()
    kT_h = nc.dram_tensor("kT_h", [D, seq], BF16, kind="Internal").ap()
    y_h = nc.dram_tensor("y_h", [seq, D], BF16, kind="Internal").ap()
    x1b_h = nc.dram_tensor("x1b_h", [seq + 128, D], BF16, kind="Internal").ap()
    yin_h = nc.dram_tensor("yin_h", [seq + 128, D], F32, kind="Internal").ap()

    es = ExitStack()
    with es:
        sem = {}
        for name in ("c0 consts "
                     "ax apet apeq apev aevq aevt aevv asp "
                     "bq bk bpes bexp bpey bdvey brs0 bself bsp "
                     "cy cxr cpet cdvet cpeao cvar crstd cx1 cx1bf ct8 ce8 "
                     "ccomb ccombTe cper cpecb cpeb cinitd cx1te cspx cspi cv ca "
                     "dw dx1 dacc dpeh dact dpep ddve dvar drstd dfin dout dv da "
                     "ax1 axw asp1 bq1 bk1 bsp1 cy1 cxr1 cspx1 cspi1 dw1 dout1 cwo "
                     "rv rpe rst rst1 rpk rio dic dgt dgv dsca dstg dpk dpk1 dg dg1").split():
            sem[name] = es.enter_context(nc.semaphore(name))

        def P2(name, i):
            return sem[name] if i % 2 == 0 else sem[name + "1"]

        idb = es.enter_context(nc.sbuf_tensor("idb", [128, 128], BF16))
        idf = es.enter_context(nc.sbuf_tensor("idf", [128, 128], F32))
        RwT = es.enter_context(nc.sbuf_tensor("RwT", [128, CC * 8], BF16))
        bfc_s = es.enter_context(nc.sbuf_tensor("bfc_s", [128, HTT * E], F32))
        bpj_s = es.enter_context(nc.sbuf_tensor("bpj_s", [8, D], BF16))
        obr = es.enter_context(nc.sbuf_tensor("obr", [128, D], F32))
        ones_b = es.enter_context(nc.sbuf_tensor("ones_b", [128, 1], BF16))
        ones_f = es.enter_context(nc.sbuf_tensor("ones_f", [1, 128], F32))
        cns = es.enter_context(nc.sbuf_tensor("cns", [128, 2], F32))
        rs_all = es.enter_context(nc.sbuf_tensor("rs_all", [128, 2 * GB], F32))
        esAB = es.enter_context(ExitStack())
        v_sb = esAB.enter_context(nc.sbuf_tensor("v_sb", [128, NKC * D], BF16))

        with ExitStack() as ps0:
            ec = ps0.enter_context
            rw_nat = ec(nc.sbuf_tensor("rw_nat", [8, D], BF16))
            bfc_nat = ec(nc.sbuf_tensor("bfc_nat", [8, H], F32))
            ob_row = ec(nc.sbuf_tensor("ob_row", [1, D], F32))
            bpj_nat = ec(nc.sbuf_tensor("bpj_nat", [8, D], F32))
            p0b = ec(nc.psum_tensor("p0b", [128, 1024], BF16))
            p0b2 = ec(nc.psum_tensor("p0b2", [128, 1024], BF16))
            p0f = ec(nc.psum_tensor("p0f", [128, 512], F32))
            p0f2 = ec(nc.psum_tensor("p0f2", [128, 512], F32))
            P0B = [p0b, p0b2]
            P0F = [p0f, p0f2]

            N_P0DMA = 6

            with nc.Block() as blk:
                @blk.gpsimd
                def _(g):
                    n0 = 0

                    def dm(out, in_):
                        nonlocal n0
                        g.dma_start(out=out, in_=in_).then_inc(sem["c0"], 16)
                        n0 += 1
                        g.wait_ge(sem["c0"], 16 * n0)

                    dm(idb[:], idb_d)
                    dm(idf[:], idf_d)
                    dm(rw_nat[:], rw_d)
                    dm(bfc_nat[:], bfc_d)
                    dm(ob_row[:], ob_d)
                    dm(bpj_nat[:], bpj_d)

                @blk.tensor
                def _(t):
                    t.wait_ge(sem["c0"], 16 * 3)
                    jb = 0
                    for cc in range(CC):
                        if jb >= 2:
                            t.wait_ge(sem["consts"], jb + 2)
                        t.transpose(P0B[jb % 2][0:128, 0:8],
                                    rw_nat[0:8, cc * 128: cc * 128 + 128],
                                    idb[0:8, 0:8]).then_inc(sem["apet"], 1)
                        jb += 1
                    NB = jb
                    t.wait_ge(sem["c0"], 16 * 4)
                    jf = 0
                    for ht in range(HTT):
                        if jf >= 2:
                            t.wait_ge(sem["consts"], NB + jf + 2)
                        t.transpose(P0F[jf % 2][0:128, 0:8],
                                    bfc_nat[0:8, ht * 128: ht * 128 + 128],
                                    idf[0:8, 0:8]).then_inc(sem["apet"], 1)
                        jf += 1
                    t.wait_ge(sem["c0"], 16 * 5)
                    t.wait_ge(sem["aevq"], 2)
                    for h in range(2):
                        if jf >= 2:
                            t.wait_ge(sem["consts"], NB + jf + 2)
                        t.matmul(P0F[jf % 2][:, 0:512], ones_f[0:1, 0:128],
                                 ob_row[0:1, h * 512:(h + 1) * 512],
                                 start=True, stop=True).then_inc(sem["apet"], 1)
                        jf += 1

                @blk.vector
                def _(v):
                    v.memset(ones_b[:], 1.0).then_inc(sem["aevq"], 1)
                    v.memset(ones_f[:], 1.0).then_inc(sem["aevq"], 1)
                    v.memset(cns[:, 0:1], -SHIFT).then_inc(sem["consts"], 1)
                    v.memset(cns[:, 1:2], LN_EPS).then_inc(sem["consts"], 1)
                    v.wait_ge(sem["c0"], 16 * 6)
                    v.tensor_copy(bpj_s[:], bpj_nat[:]).then_inc(sem["consts"], 1)
                    n = 1
                    jb = 0
                    for cc in range(CC):
                        v.wait_ge(sem["apet"], jb + 1)
                        v.tensor_copy(RwT[:, cc * 8: cc * 8 + 8],
                                      P0B[jb % 2][0:128, 0:8]).then_inc(sem["consts"], 1)
                        jb += 1
                        n += 1
                    jf = 0
                    for ht in range(HTT):
                        v.wait_ge(sem["apet"], jb + jf + 1)
                        v.tensor_copy(bfc_s[:, ht * 8: ht * 8 + 8],
                                      P0F[jf % 2][0:128, 0:8]).then_inc(sem["consts"], 1)
                        jf += 1
                        n += 1
                    for h in range(2):
                        v.wait_ge(sem["apet"], jb + jf + 1)
                        v.tensor_copy(obr[:, h * 512:(h + 1) * 512],
                                      P0F[jf % 2][:, 0:512]).then_inc(sem["consts"], 1)
                        jf += 1
                        n += 1
            CONSTS = 2 + 1 + CC + HTT + 2
            P0_PET = CC + HTT + 2

        with ExitStack() as psA:
            ec = psA.enter_context
            wqkv_s = ec(nc.sbuf_tensor("wqkv_s", [128, CC * 3 * D], BF16))
            xbf = ec(nc.sbuf_tensor("xbf", [128, 2 * 2 * D], BF16))
            xT = ec(nc.sbuf_tensor("xT", [128, 2 * CC * 256], BF16))
            qks = ec(nc.sbuf_tensor("qks", [128, 2 * 512], BF16))
            pat = ec(nc.psum_tensor("pat", [128, 1024], BF16))
            pat2 = ec(nc.psum_tensor("pat2", [128, 1024], BF16))
            paq = ec(nc.psum_tensor("paq", [128, 512], F32))
            paq2 = ec(nc.psum_tensor("paq2", [128, 512], F32))
            pav = ec(nc.psum_tensor("pav", [128, 512], F32))
            pav2 = ec(nc.psum_tensor("pav2", [128, 512], F32))
            blk = ec(nc.Block())
            PAT = [pat, pat2]
            PAQ = [paq, paq2]
            PAV = [pav, pav2]

            @blk.gpsimd
            def _(g):
                g.wait_ge(sem["consts"], CONSTS)
                g.dma_start(out=wqkv_s[:].rearrange("p (cc n) -> p cc n", cc=CC),
                            in_=_fm(wqkv_d, 0, 3 * D)).then_inc(sem["axw"], 16)
                for ga in range(GA):
                    if ga >= 2:
                        g.wait_ge(sem["apet"], P0_PET + (ga - 1) * 16)
                    g.dma_start(out=xbf[:, (ga % 2) * 2 * D:(ga % 2) * 2 * D + 2 * D]
                                .rearrange("p (t c) -> p t c", t=2),
                                in_=_tm(x_d, ga * 256, 2)).then_inc(P2("ax", ga), 16)

            @blk.sync
            def _(q):
                for ga in range(GA):
                    for m in range(16):
                        k = ga * 16 + m
                        q.wait_ge(sem["aevq"], 2 + k + 1)
                        dst = qT_h if m < 8 else kT_h
                        r0 = (m % 8) * 128
                        q.dma_start(out=dst[r0:r0 + 128, ga * 256:(ga + 1) * 256]
                                    .rearrange("p (u k) -> p u k", u=1),
                                    in_=qks[:, (k % 2) * 512:(k % 2) * 512 + 256]
                                    .rearrange("p (u k) -> p u k", u=1)).then_inc(P2("asp", k), 16)

            @blk.tensor
            def _(t):
                t.wait_ge(sem["consts"], CONSTS)
                if True:
                    t.wait_ge(sem["axw"], 16)
                for ga in range(GA):
                    t.wait_ge(P2("ax", ga), 16 * (ga // 2 + 1))
                    for tt in range(2):
                        for cc in range(CC):
                            j = ga * 16 + tt * 8 + cc
                            if j >= 2:
                                t.wait_ge(sem["aevt"], j - 1)
                            t.transpose(PAT[j % 2][:, 0:128],
                                        xbf[:, (ga % 2) * 2 * D + tt * D + cc * 128:
                                            (ga % 2) * 2 * D + tt * D + cc * 128 + 128],
                                        idb[:]).then_inc(sem["apet"], 1)
                    t.wait_ge(sem["aevt"], (ga + 1) * 16)
                    for m in range(16):
                        k = ga * 16 + m
                        if k >= 2:
                            t.wait_ge(sem["aevq"], 2 + k - 1)
                        col0 = m * 128 if m < 8 else D + (m - 8) * 128
                        for cc in range(CC):
                            mm = t.matmul(PAQ[k % 2][:, 0:256],
                                          wqkv_s[:, cc * 3 * D + col0: cc * 3 * D + col0 + 128],
                                          xT[:, (ga % 2) * CC * 256 + cc * 256:
                                              (ga % 2) * CC * 256 + cc * 256 + 256],
                                          start=(cc == 0), stop=(cc == CC - 1))
                        mm.then_inc(sem["apeq"], 1)
                    for tt in range(2):
                        for ch in range(2):
                            u = ga * 4 + tt * 2 + ch
                            if u >= 2:
                                t.wait_ge(sem["aevv"], u - 1)
                            for cc in range(CC):
                                mm = t.matmul(PAV[u % 2][:, 0:512],
                                              xT[:, (ga % 2) * CC * 256 + cc * 256 + tt * 128:
                                                  (ga % 2) * CC * 256 + cc * 256 + tt * 128 + 128],
                                              wqkv_s[:, cc * 3 * D + 2 * D + ch * 512:
                                                  cc * 3 * D + 2 * D + ch * 512 + 512],
                                              start=(cc == 0), stop=(cc == CC - 1))
                            mm.then_inc(sem["apev"], 1)

            @blk.scalar
            def _(s):
                for ga in range(GA):
                    for m in range(16):
                        k = ga * 16 + m
                        s.wait_ge(sem["apeq"], k + 1)
                        if k >= 2:
                            s.wait_ge(P2("asp", k), 16 * (k // 2))
                        s.activation(qks[:, (k % 2) * 512:(k % 2) * 512 + 256],
                                     PAQ[k % 2][:, 0:256], AF.Copy,
                                     scale=(1.0 / 32.0) if m < 8 else 1.0
                                     ).then_inc(sem["aevq"], 1)

            @blk.vector
            def _(v):
                for ga in range(GA):
                    for tt in range(2):
                        for cc in range(CC):
                            j = ga * 16 + tt * 8 + cc
                            v.wait_ge(sem["apet"], P0_PET + j + 1)
                            if j == ga * 16 and ga >= 2:
                                v.wait_ge(sem["apeq"], (ga - 1) * 16)
                                v.wait_ge(sem["apev"], (ga - 1) * 4)
                            v.tensor_copy(xT[:, (ga % 2) * CC * 256 + cc * 256 + tt * 128:
                                              (ga % 2) * CC * 256 + cc * 256 + tt * 128 + 128],
                                          PAT[j % 2][:, 0:128]).then_inc(sem["aevt"], 1)
                    for tt in range(2):
                        for ch in range(2):
                            u = ga * 4 + tt * 2 + ch
                            v.wait_ge(sem["apev"], u + 1)
                            v.tensor_copy(v_sb[:, (ga * 2 + tt) * D + ch * 512:
                                               (ga * 2 + tt) * D + ch * 512 + 512],
                                          PAV[u % 2][:, 0:512]).then_inc(sem["aevv"], 1)

        A_SP = GA * 16 * 16
        A_EVV = GA * 4
        A_EVQ_BASE = 2 + GA * 16

        with ExitStack() as psB:
            ec = psB.enter_context
            qTg = ec(nc.sbuf_tensor("qTg", [128, 2 * CC * 256], BF16))
            ktc = ec(nc.sbuf_tensor("ktc", [128, 2 * CC * 512], BF16))
            ET = ec(nc.sbuf_tensor("ET", [128, 4 * 256], BF16))
            yst = ec(nc.sbuf_tensor("yst", [128, 2 * D], BF16))
            pbs = ec(nc.psum_tensor("pbs", [128, 512], F32))
            pbs2 = ec(nc.psum_tensor("pbs2", [128, 512], F32))
            pby0 = ec(nc.psum_tensor("pby0", [128, 1024], F32))
            pby1 = ec(nc.psum_tensor("pby1", [128, 1024], F32))
            pbr = ec(nc.psum_tensor("pbr", [128, 512], F32))
            pbr2 = ec(nc.psum_tensor("pbr2", [128, 512], F32))
            blk = ec(nc.Block())
            PBS = [pbs, pbs2]
            PBY = [pby0, pby1]
            PBR = [pbr, pbr2]

            @blk.sync
            def _(q):
                q.wait_ge(sem["asp"], A_SP // 2)
                q.wait_ge(sem["asp1"], A_SP // 2)
                q.wait_ge(sem["aevv"], A_EVV)
                q.wait_ge(sem["aevq"], 2 + GA * 16)
                q.wait_ge(sem["apeq"], GA * 16)
                q.wait_ge(sem["apev"], GA * 4)
                for g in range(GB):
                    if g >= 2:
                        q.wait_ge(sem["bpes"], NKC * (g - 1))
                    q.dma_start(out=qTg[:, (g % 2) * CC * 256:(g % 2 + 1) * CC * 256]
                                .rearrange("p (cc k) -> p cc k", cc=CC),
                                in_=_fm(qT_h, g * 256, 256)).then_inc(P2("bq", g), 16)
                    for jb in range(NCH):
                        jc = g * NCH + jb
                        if jc >= 2:
                            q.wait_ge(sem["bpes"], 4 * (jc - 1))
                        q.dma_start(out=ktc[:, (jc % 2) * CC * 512:(jc % 2 + 1) * CC * 512]
                                    .rearrange("p (cc k) -> p cc k", cc=CC),
                                    in_=_fm(kT_h, jb * 512, 512)).then_inc(P2("bk", jc), 16)

            @blk.tensor
            def _(t):
                t.wait_ge(sem["aevv"], A_EVV)
                t.wait_ge(sem["aevq"], 2)

                def y_step(p):
                    g_, kc_ = divmod(p, NKC)
                    t.wait_ge(sem["bexp"], p + 1)
                    if kc_ == 0:
                        t.wait_ge(sem["bdvey"], 3 * g_)
                        t.wait_ge(sem["brs0"], g_ + 1)
                    for qt in range(2):
                        lhs = ET[:, (p % 4) * 256 + qt * 128:(p % 4) * 256 + qt * 128 + 128]
                        t.matmul(PBY[qt][:, 0:512], lhs,
                                 v_sb[:, kc_ * D: kc_ * D + 512],
                                 start=(kc_ == 0), stop=(kc_ == NKC - 1))
                        t.matmul(PBY[qt][:, 512:1024], lhs,
                                 v_sb[:, kc_ * D + 512: kc_ * D + 1024],
                                 start=(kc_ == 0), stop=(kc_ == NKC - 1))
                        t.matmul(PBR[g_ % 2][:, qt:qt + 1], lhs, ones_b[:],
                                 start=False, stop=(kc_ == NKC - 1),
                                 skip_group_check=True).then_inc(sem["bpey"], 1)

                for g in range(GB):
                    t.wait_ge(P2("bq", g), 16 * (g // 2 + 1))
                    for kc in range(NKC):
                        gkc = g * NKC + kc
                        t.wait_ge(P2("bk", g * NCH + kc // 4), 16 * ((g * NCH + kc // 4) // 2 + 1))
                        if gkc >= 2:
                            t.wait_ge(sem["bexp"], gkc - 1)
                        for cc in range(CC):
                            mm = t.matmul(
                                PBS[gkc % 2][:, 0:256],
                                ktc[:, ((g * NCH + kc // 4) % 2) * CC * 512 + cc * 512
                                    + (kc % 4) * 128:
                                    ((g * NCH + kc // 4) % 2) * CC * 512 + cc * 512
                                    + (kc % 4) * 128 + 128],
                                qTg[:, (g % 2) * CC * 256 + cc * 256:
                                    (g % 2) * CC * 256 + cc * 256 + 256],
                                start=(cc == 0), stop=(cc == CC - 1))
                        mm.then_inc(sem["bpes"], 1)
                        if gkc > 0:
                            y_step(gkc - 1)
                y_step(GB * NKC - 1)

            @blk.scalar
            def _(s):
                s.wait_ge(sem["consts"], CONSTS)
                s.memzero(PBR[0][:, 0:2]).then_inc(sem["brs0"], 1)
                s.memzero(PBR[1][:, 0:2]).then_inc(sem["brs0"], 1)

                def evac_block(g_):
                    s.wait_ge(sem["bpey"], 2 * NKC * (g_ + 1))
                    s.activation(rs_all[:, 2 * g_: 2 * g_ + 2],
                                 PBR[g_ % 2][:, 0:2], AF.Copy).then_inc(sem["bdvey"], 1)
                    for qt in range(2):
                        s_ = 2 * g_ + qt
                        if s_ >= 2:
                            s.wait_ge(P2("bsp", s_), 16 * (s_ // 2))
                        s.activation(yst[:, (s_ % 2) * D:(s_ % 2) * D + D],
                                     PBY[qt][:, 0:1024], AF.Copy).then_inc(sem["bdvey"], 1)
                    if g_ + 2 < GB:
                        s.wait_ge(sem["bdvey"], 3 * g_ + 1)
                        s.memzero(PBR[g_ % 2][:, 0:2]).then_inc(sem["brs0"], 1)

                for g in range(GB):
                    for kc in range(NKC):
                        gkc = g * NKC + kc
                        s.wait_ge(sem["bpes"], gkc + 1)
                        if gkc >= 4:
                            s.wait_ge(sem["bpey"], 2 * (gkc - 3))
                        s.activation(ET[:, (gkc % 4) * 256:(gkc % 4) * 256 + 256],
                                     PBS[gkc % 2][:, 0:256], AF.Exp,
                                     bias=cns[:, 0:1]).then_inc(sem["bexp"], 1)
                        if kc == 0 and g >= 1:
                            evac_block(g - 1)
                evac_block(GB - 1)



            @blk.gpsimd
            def _(g_):
                for g in range(GB):
                    for qt in range(2):
                        s_ = 2 * g + qt
                        g_.wait_ge(sem["bdvey"], 3 * g + qt + 2)
                        g_.dma_start(out=y_h[s_ * 128:(s_ + 1) * 128, :],
                                     in_=yst[:, (s_ % 2) * D:(s_ % 2) * D + D]
                                     ).then_inc(P2("bsp", s_), 16)

        B_SP = 16 * 2 * GB
        esAB.close()

        esCD = es.enter_context(ExitStack())
        wfcq = esCD.enter_context(nc.sbuf_tensor("wfcq", [128, 2 * CC * (H // 4)], BF16))
        wpjq = esCD.enter_context(nc.sbuf_tensor("wpjq", [128, (H // 4 // 128) * D], BF16))

        with ExitStack() as psC:
            ec = psC.enter_context
            WoT = ec(nc.sbuf_tensor("WoT", [128, CC * D], BF16))
            wo_nat = ec(nc.sbuf_tensor("wo_nat", [128, CC * D], BF16))
            ybin = ec(nc.sbuf_tensor("ybin", [128, 2 * D], BF16))
            xres = ec(nc.sbuf_tensor("xres", [128, 2 * D], F32))
            yT = ec(nc.sbuf_tensor("yT", [128, 2 * CC * 128], BF16))
            v2 = ec(nc.sbuf_tensor("v2", [128, 2 * D], F32))
            x1f = ec(nc.sbuf_tensor("x1f", [128, 2 * D], F32))
            x1b = ec(nc.sbuf_tensor("x1b", [128, 2 * D], BF16))
            x1Ts = ec(nc.sbuf_tensor("x1Ts", [128, 2 * CC * 512], BF16))
            sqsc = ec(nc.sbuf_tensor("sqsc", [128, 2 * D], F32))
            lns = ec(nc.sbuf_tensor("lns", [128, 16], F32))
            r8 = ec(nc.sbuf_tensor("r8", [128, 2 * 8], F32))
            e8 = ec(nc.sbuf_tensor("e8", [128, 2 * 8], F32))
            rsc = ec(nc.sbuf_tensor("rsc", [128, 2 * 8], F32))
            rsc2 = ec(nc.sbuf_tensor("rsc2", [128, 2 * 8], F32))
            rsm = ec(nc.sbuf_tensor("rsm", [128, 8], F32))
            cTb = ec(nc.sbuf_tensor("cTb", [8, 2 * 128], BF16))
            inis = ec(nc.sbuf_tensor("inis", [128, 2 * D], F32))
            comb = ec(nc.sbuf_tensor("comb", [128, NT * 8], F32))
            Mall = ec(nc.sbuf_tensor("Mall", [128, NT * 8], F32))
            MT8 = ec(nc.sbuf_tensor("MT8", [8, 2 * 128], F32))
            PS8 = ec(nc.sbuf_tensor("PS8", [8, 2 * 128], F32))
            EX8 = ec(nc.sbuf_tensor("EX8", [8, 2 * 128], F32))
            MW8 = ec(nc.sbuf_tensor("MW8", [8, 2 * 128], F32))
            z8 = ec(nc.sbuf_tensor("z8", [8, 128], F32))
            rb8 = ec(nc.sbuf_tensor("rb8", [8, 2], F32))
            ecap = ec(nc.sbuf_tensor("ecap", [8, 2], F32))
            stk = ec(nc.sbuf_tensor("stk", [128, 2 * 8], F32))
            msk = ec(nc.sbuf_tensor("msk", [128, 2 * 8], F32))
            msk2 = ec(nc.sbuf_tensor("msk2", [128, 2 * 8], F32))
            sv = ec(nc.sbuf_tensor("sv", [128, 2 * 8], F32))
            p1s = ec(nc.sbuf_tensor("p1s", [128, 2 * 2], F32))
            p2s = ec(nc.sbuf_tensor("p2s", [128, 2 * 2], F32))
            si32 = ec(nc.sbuf_tensor("si32", [128, 2 * 2], mybir.dt.int32))
            io128 = ec(nc.sbuf_tensor("io128", [128, 1], mybir.dt.int32))
            io128f = ec(nc.sbuf_tensor("io128f", [128, 1], F32))
            io8 = ec(nc.sbuf_tensor("io8", [8, 1], mybir.dt.int32))
            zpk = ec(nc.sbuf_tensor("zpk", [128, (8 * (seq // 4 + 512)) * 2 // 128], F32))
            zrow = ec(nc.sbuf_tensor("zrow", [128, D], BF16))
            pct = ec(nc.psum_tensor("pct", [128, 1024], BF16))
            pct2 = ec(nc.psum_tensor("pct2", [128, 1024], BF16))
            pcao = ec(nc.psum_tensor("pcao", [128, 1024], F32))
            pcao2 = ec(nc.psum_tensor("pcao2", [128, 1024], F32))
            pcr = ec(nc.psum_tensor("pcr", [128, 512], F32))
            pccb = ec(nc.psum_tensor("pccb", [128, 512], F32))
            blk = nc.Block()
            blk.__enter__()
            PCT = [pct, pct2]
            PAO = [pcao, pcao2]

            CV0 = 64

            CVM = {}
            ev_cv = []

            @blk.vector
            def _(v):
                nv = 0
                prev = {}

                def emit(ti, name, waits, buildfn, is_ev=False):
                    nonlocal nv
                    for w in waits:
                        w()
                    if ti in prev:
                        v.wait_ge(sem["cv"], prev[ti])
                    buildfn().then_inc(sem["cv"], 1)
                    nv += 1
                    prev[ti] = nv
                    CVM[(ti, name)] = nv
                    if is_ev:
                        ev_cv.append(nv)

                for cc in range(CC):
                    for dd in range(CC):
                        jw = cc * CC + dd
                        v.wait_ge(sem["cpet"], jw + 1)
                        v.tensor_copy(WoT[:, dd * D + cc * 128: dd * D + cc * 128 + 128],
                                      PCT[jw % 2][:, 0:128]).then_inc(sem["cv"], 1)
                        nv += 1
                v.wait_ge(sem["rio"], 4)
                zpk3 = zpk[:].rearrange("p (f c) -> p f c", c=2)
                emit(-1, "zpk1", [], lambda: v.tensor_scalar(
                    zpk3[:, :, 0:1], zpk3[:, :, 0:1], float(seq), None, ALU.add))
                emit(-2, "z8", [], lambda: v.memset(z8[:], 0.0))
                emit(-2, "rb", [], lambda: v.memset(rb8[0:8, 0:1], 0.0))
                emit(-2, "iof", [], lambda: v.tensor_copy(io128f[:], io128[:]))
                emit(-2, "ef", [], lambda: v.tensor_copy(MT8[0:8, 0:1], io8[0:8, 0:1]))
                emit(-2, "ecap0", [], lambda: v.tensor_scalar(
                    ecap[0:8, 0:1], MT8[0:8, 0:1], float(CAP), None, ALU.mult))
                emit(-2, "ecap1", [], lambda: v.tensor_scalar(
                    ecap[0:8, 1:2], ecap[0:8, 0:1], float(CAP), None, ALU.add))
                CVM[("pre", "done")] = nv

                def prA(ti):
                    o2 = ti % 2
                    mt = MT8[0:8, o2 * 128: o2 * 128 + 128]
                    ps = PS8[0:8, o2 * 128: o2 * 128 + 128]
                    ex = EX8[0:8, o2 * 128: o2 * 128 + 128]
                    mw = MW8[0:8, o2 * 128: o2 * 128 + 128]
                    cmb = comb[:, ti * 8: ti * 8 + 8]
                    emit(ti, "MT", [lambda: v.wait_ge(sem["rpe"], 2 * ti + 1)],
                         lambda: v.tensor_copy(mt, PAO[ti % 2][0:8, 0:128]))
                    emit(ti, "scan", [], lambda: v.tensor_tensor_scan(
                        ps, mt, z8[0:8, :], rb8[0:8, ti % 2: ti % 2 + 1],
                        ALU.add, ALU.add))
                    emit(ti, "rbn", [], lambda: v.tensor_copy(
                        rb8[0:8, (ti + 1) % 2: (ti + 1) % 2 + 1], ps[0:8, 127:128]))
                    emit(ti, "ex", [], lambda: v.tensor_sub(ex, ps, mt))
                    emit(ti, "slotA", [], lambda: v.tensor_scalar(
                        ps, ex, ecap[0:8, 0:1], None, ALU.add))
                    emit(ti, "mwith", [], lambda: v.tensor_scalar(
                        mw, ps, ecap[0:8, 1:2], None, ALU.is_lt))
                    emit(ti, "M2", [], lambda: v.tensor_mul(mw, mw, mt))
                    emit(ti, "sf1", [], lambda: v.tensor_scalar(
                        ex, ps, -float(NSLOT + ti), None, ALU.add))
                    emit(ti, "sf2", [], lambda: v.tensor_mul(ex, ex, mw))
                    emit(ti, "sf3", [], lambda: v.tensor_scalar(
                        ex, ex, float(NSLOT + ti), None, ALU.add))
                    emit(ti, "stk", [lambda: v.wait_ge(sem["rpe"], 2 * ti + 2)],
                         lambda: v.tensor_copy(stk[:, o2 * 8: o2 * 8 + 8],
                                               (pcr if o2 == 0 else pccb)[0:128, 0:8]))
                    emit(ti, "s1", [], lambda: v.tensor_reduce(
                        sv[:, o2 * 8 + 0: o2 * 8 + 1], stk[:, o2 * 8: o2 * 8 + 8],
                        axis=AX, op=ALU.min))
                    emit(ti, "m1", [], lambda: v.tensor_scalar(
                        msk[:, o2 * 8: o2 * 8 + 8], stk[:, o2 * 8: o2 * 8 + 8],
                        sv[:, o2 * 8 + 0: o2 * 8 + 1], None, ALU.is_equal))
                    emit(ti, "cm1", [], lambda: v.tensor_mul(
                        msk2[:, o2 * 8: o2 * 8 + 8], cmb, msk[:, o2 * 8: o2 * 8 + 8]))
                    emit(ti, "c1", [], lambda: v.reduce_sum(
                        sv[:, o2 * 8 + 2: o2 * 8 + 3], msk2[:, o2 * 8: o2 * 8 + 8],
                        axis=AX))
                    emit(ti, "stm", [], lambda: v.scalar_tensor_tensor(
                        stk[:, o2 * 8: o2 * 8 + 8], msk[:, o2 * 8: o2 * 8 + 8], BIGS,
                        stk[:, o2 * 8: o2 * 8 + 8], ALU.mult, ALU.add))
                    emit(ti, "s2", [], lambda: v.tensor_reduce(
                        sv[:, o2 * 8 + 1: o2 * 8 + 2], stk[:, o2 * 8: o2 * 8 + 8],
                        axis=AX, op=ALU.min))
                    emit(ti, "m2", [], lambda: v.tensor_scalar(
                        msk[:, o2 * 8: o2 * 8 + 8], stk[:, o2 * 8: o2 * 8 + 8],
                        sv[:, o2 * 8 + 1: o2 * 8 + 2], None, ALU.is_equal))
                    emit(ti, "cm2", [], lambda: v.tensor_mul(
                        msk2[:, o2 * 8: o2 * 8 + 8], cmb, msk[:, o2 * 8: o2 * 8 + 8]))
                    emit(ti, "c2", [], lambda: v.reduce_sum(
                        sv[:, o2 * 8 + 3: o2 * 8 + 4], msk2[:, o2 * 8: o2 * 8 + 8],
                        axis=AX))
                    emit(ti, "tokf", [], lambda: v.tensor_scalar(
                        sv[:, o2 * 8 + 4: o2 * 8 + 5], io128f[:],
                        float(ti * 128), None, ALU.add))
                    pw_ = []
                    if ti >= 2:
                        pw_.append(lambda: v.wait_ge(P2("rst", ti), 32 * (ti // 2)))
                    emit(ti, "p1a", pw_, lambda: v.tensor_copy(
                        p1s[:, o2 * 2: o2 * 2 + 1], sv[:, o2 * 8 + 4: o2 * 8 + 5]))
                    emit(ti, "p1b", [], lambda: v.tensor_copy(
                        p1s[:, o2 * 2 + 1: o2 * 2 + 2], sv[:, o2 * 8 + 2: o2 * 8 + 3]))
                    emit(ti, "p2a", [], lambda: v.tensor_copy(
                        p2s[:, o2 * 2: o2 * 2 + 1], sv[:, o2 * 8 + 4: o2 * 8 + 5]))
                    emit(ti, "p2b", [], lambda: v.tensor_copy(
                        p2s[:, o2 * 2 + 1: o2 * 2 + 2], sv[:, o2 * 8 + 3: o2 * 8 + 4]))
                    emit(ti, "si1", [], lambda: v.tensor_copy(
                        si32[:, o2 * 2: o2 * 2 + 1], sv[:, o2 * 8 + 0: o2 * 8 + 1]))
                    emit(ti, "si2", [], lambda: v.tensor_copy(
                        si32[:, o2 * 2 + 1: o2 * 2 + 2], sv[:, o2 * 8 + 1: o2 * 8 + 2]))

                def recip(ti):
                    o2 = ti % 2
                    emit(ti, "recip", [], lambda: v.reciprocal(
                        lns[:, o2 * 8 + 7: o2 * 8 + 8], rs_all[:, ti: ti + 1]))

                def yTev(ti):
                    o2 = ti % 2
                    for dd in range(CC):
                        q = len(ev_cv)
                        emit(ti, f"yT{dd}",
                             [lambda q=q: v.wait_ge(sem["cpet"], CV0 + q + 1)],
                             lambda dd=dd, q=q: v.tensor_copy(
                                 yT[:, o2 * CC * 128 + dd * 128:
                                    o2 * CC * 128 + dd * 128 + 128],
                                 PCT[q % 2][:, 0:128]), is_ev=True)

                def resx1(ti):
                    o2 = ti % 2
                    emit(ti, "res",
                         [lambda: v.wait_ge(sem["cpeao"], ti + 1),
                          lambda: v.wait_ge(P2("cxr", ti), 16 * (ti // 2 + 1))],
                         lambda: v.scalar_tensor_tensor(
                             v2[:, o2 * D: o2 * D + D], PAO[ti % 2][:, 0:1024],
                             lns[:, o2 * 8 + 7: o2 * 8 + 8],
                             xres[:, o2 * D: o2 * D + D], ALU.mult, ALU.add))
                    emit(ti, "obr", [], lambda: v.tensor_add(
                        v2[:, o2 * D: o2 * D + D], v2[:, o2 * D: o2 * D + D], obr[:]))
                    emit(ti, "sum", [], lambda: v.reduce_sum(
                        lns[:, o2 * 8 + 0: o2 * 8 + 1], v2[:, o2 * D: o2 * D + D],
                        axis=AX))
                    emit(ti, "sq", [], lambda: v.tensor_mul(
                        sqsc[:, o2 * D: o2 * D + D], v2[:, o2 * D: o2 * D + D],
                        v2[:, o2 * D: o2 * D + D]))
                    emit(ti, "ssum", [], lambda: v.reduce_sum(
                        lns[:, o2 * 8 + 1: o2 * 8 + 2], sqsc[:, o2 * D: o2 * D + D],
                        axis=AX))
                    emit(ti, "mean", [], lambda: v.tensor_scalar(
                        lns[:, o2 * 8 + 2: o2 * 8 + 3], lns[:, o2 * 8 + 0: o2 * 8 + 1],
                        1.0 / D, None, ALU.mult))
                    emit(ti, "mean2", [], lambda: v.tensor_mul(
                        lns[:, o2 * 8 + 3: o2 * 8 + 4], lns[:, o2 * 8 + 2: o2 * 8 + 3],
                        lns[:, o2 * 8 + 2: o2 * 8 + 3]))
                    emit(ti, "var", [], lambda: v.scalar_tensor_tensor(
                        lns[:, o2 * 8 + 4: o2 * 8 + 5], lns[:, o2 * 8 + 1: o2 * 8 + 2],
                        1.0 / D, lns[:, o2 * 8 + 3: o2 * 8 + 4],
                        ALU.mult, ALU.subtract))
                    emit(ti, "x1", [lambda: v.wait_ge(sem["crstd"], ti + 1)],
                         lambda: v.tensor_scalar(
                             x1f[:, o2 * D: o2 * D + D], v2[:, o2 * D: o2 * D + D],
                             lns[:, o2 * 8 + 2: o2 * 8 + 3],
                             lns[:, o2 * 8 + 5: o2 * 8 + 6],
                             ALU.subtract, ALU.mult))

                def x1Tev(ti):
                    for cc in range(CC):
                        q = len(ev_cv)
                        emit(ti, f"x1T{cc}",
                             [lambda q=q: v.wait_ge(sem["cpet"], CV0 + q + 1)],
                             lambda cc=cc, q=q: v.tensor_copy(
                                 x1Ts[:, ((ti // 4) % 2) * CC * 512 + cc * 512
                                      + (ti % 4) * 128:
                                      ((ti // 4) % 2) * CC * 512 + cc * 512
                                      + (ti % 4) * 128 + 128],
                                 PCT[q % 2][:, 0:128]), is_ev=True)

                def router(ti):
                    o2 = ti % 2
                    emit(ti, "rmax", [lambda: v.wait_ge(sem["cper"], ti + 1)],
                         lambda: v.reduce_max(rsm[:, o2 * 4 + 0: o2 * 4 + 1],
                                              pcr[:, 0:8], axis=AX))
                    emit(ti, "t8", [], lambda: v.tensor_scalar(
                        r8[:, o2 * 8: o2 * 8 + 8], pcr[:, 0:8],
                        rsm[:, o2 * 4 + 0: o2 * 4 + 1], None, ALU.subtract))
                    emit(ti, "mask1", [], lambda: v.tensor_scalar(
                        rsc[:, o2 * 8: o2 * 8 + 8], r8[:, o2 * 8: o2 * 8 + 8],
                        0.0, None, ALU.is_equal))
                    emit(ti, "inv1", [], lambda: v.tensor_scalar(
                        rsc2[:, o2 * 8: o2 * 8 + 8], rsc[:, o2 * 8: o2 * 8 + 8],
                        -1.0, 1.0, ALU.mult, ALU.add))
                    emit(ti, "pw", [lambda: v.wait_ge(sem["ce8"], ti + 1)],
                         lambda: v.tensor_mul(rsc2[:, o2 * 8: o2 * 8 + 8],
                                              e8[:, o2 * 8: o2 * 8 + 8],
                                              rsc2[:, o2 * 8: o2 * 8 + 8]))
                    emit(ti, "second", [], lambda: v.reduce_max(
                        rsm[:, o2 * 4 + 1: o2 * 4 + 2], rsc2[:, o2 * 8: o2 * 8 + 8],
                        axis=AX))
                    emit(ti, "mask2", [], lambda: v.tensor_scalar(
                        rsc2[:, o2 * 8: o2 * 8 + 8], rsc2[:, o2 * 8: o2 * 8 + 8],
                        rsm[:, o2 * 4 + 1: o2 * 4 + 2], None, ALU.is_equal))
                    emit(ti, "wsum", [], lambda: v.tensor_add(
                        Mall[:, ti * 8: ti * 8 + 8], rsc[:, o2 * 8: o2 * 8 + 8],
                        rsc2[:, o2 * 8: o2 * 8 + 8]))
                    emit(ti, "cu", [], lambda: v.tensor_mul(
                        rsc[:, o2 * 8: o2 * 8 + 8], e8[:, o2 * 8: o2 * 8 + 8],
                        Mall[:, ti * 8: ti * 8 + 8]))
                    emit(ti, "den", [], lambda: v.tensor_scalar(
                        rsm[:, o2 * 4 + 2: o2 * 4 + 3], rsm[:, o2 * 4 + 1: o2 * 4 + 2],
                        1.0, None, ALU.add))
                    emit(ti, "r2", [], lambda: v.reciprocal(
                        rsm[:, o2 * 4 + 3: o2 * 4 + 4], rsm[:, o2 * 4 + 2: o2 * 4 + 3]))
                    emit(ti, "comb", [], lambda: v.tensor_scalar(
                        comb[:, ti * 8: ti * 8 + 8], rsc[:, o2 * 8: o2 * 8 + 8],
                        rsm[:, o2 * 4 + 3: o2 * 4 + 4], None, ALU.mult))

                def combTe(ti):
                    o2 = ti % 2
                    cw = [lambda: v.wait_ge(sem["cpecb"], ti + 1)]
                    if ti >= 2:
                        cw.append(lambda: v.wait_ge(sem["cpeb"], 2 * (ti - 1)))
                    emit(ti, "combTe", cw, lambda: v.tensor_copy(
                        cTb[0:8, o2 * 128: o2 * 128 + 128], pccb[0:8, 0:128]))

                def initf(ti):
                    o2 = ti % 2
                    iw = [lambda: v.wait_ge(sem["cpeb"], 2 * (ti + 1))]
                    if ti >= 2:
                        iw.append(lambda: v.wait_ge(P2("cspi", ti), 16 * (ti // 2)))
                    emit(ti, "init", iw, lambda: v.tensor_add(
                        inis[:, o2 * D: o2 * D + D], x1f[:, o2 * D: o2 * D + D],
                        PAO[ti % 2][:, 0:1024]))

                for g in range(NT // 2):
                    a, b = 2 * g, 2 * g + 1
                    recip(a)
                    yTev(a)
                    recip(b)
                    yTev(b)
                    resx1(a)
                    resx1(b)
                    x1Tev(a)
                    x1Tev(b)
                    router(a)
                    router(b)
                    combTe(a)
                    combTe(b)
                    initf(a)
                    initf(b)
                    prA(a)
                    prA(b)
                CVM[("all", "done")] = nv

            CV_TOT = CVM[("all", "done")]
            CV_PRE = CVM[("pre", "done")]

            def TPy(ti):
                return CV0 + (ti // 2) * 32 + (8 if ti % 2 == 0 else 16)

            @blk.sync
            def _(q):
                q.wait_ge(sem["bsp"], 16 * GB)
                q.wait_ge(sem["bsp1"], 16 * GB)
                q.wait_ge(sem["bexp"], GB * NKC)
                q.wait_ge(sem["bdvey"], 3 * GB)
                q.wait_ge(sem["brs0"], GB)
                for ti in range(NT):
                    q.wait_ge(P2("bsp", ti), 16 * (ti // 2 + 1))
                    if ti >= 2:
                        q.wait_ge(sem["cpet"], TPy(ti - 2))
                    q.dma_start(out=ybin[:, (ti % 2) * D:(ti % 2) * D + D],
                                in_=y_h[ti * 128:(ti + 1) * 128, :]).then_inc(P2("cy", ti), 16)
                    if ti >= 2:
                        q.wait_ge(sem["cv"], CVM[(ti - 2, "res")])
                    q.dma_start(out=xres[:, (ti % 2) * D:(ti % 2) * D + D]
                                .rearrange("p (u c) -> p u c", u=1),
                                in_=_tm(x_d, ti * 128, 1)).then_inc(P2("cxr", ti), 16)

            @blk.tensor
            def _(t):
                t.wait_ge(sem["bdvey"], 3 * GB)
                t.wait_ge(sem["bexp"], GB * NKC)
                t.wait_ge(sem["cwo"], 16)
                for cc in range(CC):
                    for dd in range(CC):
                        jw = cc * CC + dd
                        if jw >= 2:
                            t.wait_ge(sem["cv"], jw - 1)
                        t.transpose(PCT[jw % 2][:, 0:128],
                                    wo_nat[:, cc * D + dd * 128: cc * D + dd * 128 + 128],
                                    idb[:]).then_inc(sem["cpet"], 1)
                qn = [0]

                def trans(src_ap):
                    q = qn[0]
                    t.wait_ge(sem["cv"], ev_cv[q - 2] if q >= 2 else CV0 + q - 1)
                    t.transpose(PCT[q % 2][:, 0:128], src_ap, idb[:]).then_inc(sem["cpet"], 1)
                    qn[0] += 1

                def yTt(ti):
                    t.wait_ge(P2("cy", ti), 16 * (ti // 2 + 1))
                    for dd in range(CC):
                        trans(ybin[:, (ti % 2) * D + dd * 128:(ti % 2) * D + dd * 128 + 128])

                def ao(ti):
                    t.wait_ge(sem["cv"], CVM[(ti, "yT7")])
                    if ti >= 2:
                        t.wait_ge(sem["cv"], CVM[(ti - 2, "MT")])
                    for ch in range(2):
                        for dd in range(CC):
                            mm = t.matmul(PAO[ti % 2][:, ch * 512: ch * 512 + 512],
                                          yT[:, (ti % 2) * CC * 128 + dd * 128:
                                              (ti % 2) * CC * 128 + dd * 128 + 128],
                                          WoT[:, dd * D + ch * 512: dd * D + ch * 512 + 512],
                                          start=(dd == 0), stop=(dd == CC - 1))
                    mm.then_inc(sem["cpeao"], 1)

                def x1Tt(ti):
                    t.wait_ge(sem["cx1bf"], ti + 1)
                    for cc in range(CC):
                        trans(x1b[:, (ti % 2) * D + cc * 128:(ti % 2) * D + cc * 128 + 128])

                def router(ti):
                    t.wait_ge(sem["cv"], CVM[(ti, "x1T7")])
                    if ti >= 1:
                        t.wait_ge(sem["cv"], CVM[(ti - 1, "t8")])
                    pe_ = 2 * (ti // 2) - 2
                    if pe_ >= 0:
                        t.wait_ge(sem["cv"], CVM[(pe_, "stk")])
                    for cc in range(CC):
                        mm = t.matmul(pcr[:, 0:8],
                                      x1Ts[:, ((ti // 4) % 2) * CC * 512 + cc * 512
                                          + (ti % 4) * 128:
                                          ((ti // 4) % 2) * CC * 512 + cc * 512
                                          + (ti % 4) * 128 + 128],
                                      RwT[:, cc * 8: cc * 8 + 8],
                                      start=(cc == 0), stop=(cc == CC - 1))
                    mm.then_inc(sem["cper"], 1)

                def combt(ti):
                    t.wait_ge(sem["cv"], CVM[(ti, "comb")])
                    if ti >= 1:
                        t.wait_ge(sem["cv"], CVM[(ti - 1, "combTe")])
                    po_ = ti - 1 if ti % 2 == 0 else ti - 2
                    if po_ >= 0:
                        t.wait_ge(sem["cv"], CVM[(po_, "stk")])
                    t.transpose(pccb[0:8, 0:128], comb[:, ti * 8: ti * 8 + 8],
                                idf[:]).then_inc(sem["cpecb"], 1)

                def bm(ti):
                    t.wait_ge(sem["cv"], CVM[(ti, "combTe")])
                    for ch in range(2):
                        t.matmul(PAO[ti % 2][:, ch * 512: ch * 512 + 512],
                                 cTb[0:8, (ti % 2) * 128:(ti % 2) * 128 + 128],
                                 bpj_s[0:8, ch * 512: ch * 512 + 512],
                                 start=True, stop=True).then_inc(sem["cpeb"], 1)

                def prT(ti):
                    t.wait_ge(sem["cv"], CVM[(ti, "wsum")])
                    t.wait_ge(sem["cv"], CVM[(ti, "init")])
                    t.transpose(PAO[ti % 2][0:8, 0:128], Mall[:, ti * 8: ti * 8 + 8],
                                idf[:]).then_inc(sem["rpe"], 1)
                    t.wait_ge(sem["cv"], CVM[(ti, "sf3")])
                    if ti % 2 == 0:
                        t.wait_ge(sem["cv"], CVM[(ti + 1, "t8")])
                    else:
                        t.wait_ge(sem["cv"], CVM[(ti, "combTe")])
                    t.transpose((pcr if ti % 2 == 0 else pccb)[0:128, 0:8],
                                EX8[0:8, (ti % 2) * 128: (ti % 2) * 128 + 128],
                                idf[0:8, 0:8]).then_inc(sem["rpe"], 1)

                for g in range(NT // 2):
                    a, b = 2 * g, 2 * g + 1
                    yTt(a)
                    yTt(b)
                    ao(a)
                    ao(b)
                    x1Tt(a)
                    x1Tt(b)
                    router(a)
                    router(b)
                    combt(a)
                    combt(b)
                    bm(a)
                    bm(b)
                    prT(a)
                    prT(b)

            @blk.scalar
            def _(s):
                s.wait_ge(sem["consts"], CONSTS)
                na = [0]

                def rstd_cast(ti):
                    s.wait_ge(sem["cv"], CVM[(ti, "var")])
                    s.activation(lns[:, (ti % 2) * 8 + 6:(ti % 2) * 8 + 7],
                                 lns[:, (ti % 2) * 8 + 4:(ti % 2) * 8 + 5],
                                 AF.Ln, bias=cns[:, 1:2]).then_inc(sem["ca"], 1)
                    na[0] += 1
                    s.wait_ge(sem["ca"], na[0])
                    s.activation(lns[:, (ti % 2) * 8 + 5:(ti % 2) * 8 + 6],
                                 lns[:, (ti % 2) * 8 + 6:(ti % 2) * 8 + 7],
                                 AF.Exp, scale=-0.5).then_inc(sem["crstd"], 1)
                    s.wait_ge(sem["cv"], CVM[(ti, "x1")])
                    if ti >= 2:
                        s.wait_ge(P2("cspx", ti), 16 * (ti // 2))
                    s.activation(x1b[:, (ti % 2) * D:(ti % 2) * D + D],
                                 x1f[:, (ti % 2) * D:(ti % 2) * D + D],
                                 AF.Copy).then_inc(sem["cx1bf"], 1)

                def e8f(ti):
                    s.wait_ge(sem["cv"], CVM[(ti, "t8")])
                    s.activation(e8[:, (ti % 2) * 8:(ti % 2) * 8 + 8],
                                 r8[:, (ti % 2) * 8:(ti % 2) * 8 + 8],
                                 AF.Exp).then_inc(sem["ce8"], 1)

                for g in range(NT // 2):
                    a, b = 2 * g, 2 * g + 1
                    rstd_cast(a)
                    rstd_cast(b)
                    e8f(a)
                    e8f(b)

            RPK_TOT = 16 * 2
            HQ_ = H // 4
            HT_ = HQ_ // 128

            @blk.gpsimd
            def _(g_):
                g_.memset(zrow[:], 0.0).then_inc(sem["rio"], 1)
                g_.memset(zpk[:], 0.0).then_inc(sem["rio"], 1)
                g_.iota(io128[:], [[0, 1]], channel_multiplier=1,
                        allow_small_or_imprecise_dtypes=True).then_inc(sem["rio"], 1)
                g_.iota(io8[0:8, 0:1], [[0, 1]], channel_multiplier=1,
                        allow_small_or_imprecise_dtypes=True).then_inc(sem["rio"], 1)
                g_.wait_ge(sem["bsp"], 16 * GB)
                g_.wait_ge(sem["bsp1"], 16 * GB)
                g_.wait_ge(sem["bdvey"], 3 * GB)
                g_.dma_start(out=wo_nat[:].rearrange("p (cc d) -> p cc d", cc=CC),
                             in_=_fm(wo_d, 0, D)).then_inc(sem["cwo"], 16)
                g_.dma_start(out=wfcq[:, 0:CC * HQ_]
                             .rearrange("p (cc h) -> p cc h", cc=CC),
                             in_=wfc_d[0:D, 0:HQ_]
                             .rearrange("(cc p) h -> p cc h", p=128)
                             ).then_inc(sem["dw"], 16)
                g_.dma_start(out=wpjq[:].rearrange("p (ht c) -> p ht c", ht=HT_),
                             in_=wpj_d[0:HQ_, :]
                             .rearrange("(ht p) c -> p ht c", p=128)
                             ).then_inc(sem["dpk"], 16)
                g_.dma_start(out=x1b_h[seq:seq + 128, :],
                             in_=zrow[:]).then_inc(sem["rpk"], 16)
                g_.wait_ge(sem["cv"], CV_PRE)
                g_.dma_start(out=pck_h[0:NSLOT, :].rearrange("(p f) c -> p (f c)", p=128),
                             in_=zpk[:]).then_inc(sem["rpk"], 16)
                for ti in range(NT):
                    o2 = ti % 2
                    g_.wait_ge(sem["cx1bf"], ti + 1)
                    g_.dma_start(out=x1b_h[ti * 128:(ti + 1) * 128, :],
                                 in_=x1b[:, o2 * D: o2 * D + D]
                                 ).then_inc(P2("cspx", ti), 16)
                    g_.wait_ge(sem["cv"], CVM[(ti, "init")])
                    g_.dma_start(out=yin_h[ti * 128:(ti + 1) * 128, :],
                                 in_=inis[:, o2 * D: o2 * D + D]
                                 ).then_inc(P2("cspi", ti), 16)
                    g_.wait_ge(sem["cv"], CVM[(ti, "si2")])
                    g_.wait_ge(sem["rpk"], 16)
                    g_.indirect_dma_start(
                        out=pck_h, out_offset=bass.IndirectOffsetOnAxis(
                            si32[:, o2 * 2: o2 * 2 + 1], 0),
                        in_=p1s[:, o2 * 2: o2 * 2 + 2], in_offset=None,
                        ).then_inc(P2("rst", ti), 16)
                    g_.indirect_dma_start(
                        out=pck_h, out_offset=bass.IndirectOffsetOnAxis(
                            si32[:, o2 * 2 + 1: o2 * 2 + 2], 0),
                        in_=p2s[:, o2 * 2: o2 * 2 + 2], in_offset=None,
                        ).then_inc(P2("rst", ti), 16)
            blk.__exit__(None, None, None)

        C_SPX = 16 * (NT // 4)
        C_SPI = 16 * NT

        FC_PER_E = 4 * SGE * HT
        PJ_PER_E = 4 * SGE * 8 // 4
        EV_PER_E = 4 * SGE * 8
        with ExitStack() as psD:
            ec = psD.enter_context
            x1gT = ec(nc.sbuf_tensor("x1gT", [128, CC * CAP], BF16))
            acc = ec(nc.sbuf_tensor("acc", [128, STE * D], F32))
            hTs = ec(nc.sbuf_tensor("hT", [128, 2 * HT * 512], BF16))
            x1g = ec(nc.sbuf_tensor("x1g", [128, 6 * D], BF16))
            pcke = ec(nc.sbuf_tensor("pcke", [128, 2 * STE * 2], F32))
            idxe = ec(nc.sbuf_tensor("idxe", [128, E * STE], mybir.dt.int32))
            csle = ec(nc.sbuf_tensor("csle", [128, E * STE], F32))
            pdh = ec(nc.psum_tensor("pdh", [128, 512], F32))
            pdh2 = ec(nc.psum_tensor("pdh2", [128, 512], F32))
            pdp = ec(nc.psum_tensor("pdp", [128, 512], F32))
            pdp2 = ec(nc.psum_tensor("pdp2", [128, 512], F32))
            pdt = ec(nc.psum_tensor("pdt", [128, 1024], BF16))
            pdt2 = ec(nc.psum_tensor("pdt2", [128, 1024], BF16))
            pdt3 = ec(nc.psum_tensor("pdt3", [128, 1024], BF16))
            pdt4 = ec(nc.psum_tensor("pdt4", [128, 1024], BF16))
            blk = ec(nc.Block())
            PDH = [pdh, pdh2]
            PDP = [pdp, pdp2]
            PDT = [pdt, pdt2, pdt3, pdt4]

            sched = [("f", 0), ("f", 1)]
            for i in range(SGE - 2):
                sched += [("p", i), ("f", i + 2)]
            sched += [("p", SGE - 2), ("p", SGE - 1)]

            @blk.gpsimd
            def _(g_):
                g_.wait_ge(sem["cv"], CV_TOT)
                g_.wait_ge(sem["rpk"], RPK_TOT)
                g_.wait_ge(sem["rst"], 32 * (NT // 2))
                g_.wait_ge(sem["rst1"], 32 * (NT // 2))
                g_.wait_ge(sem["cspx"], 16 * (NT // 2))
                g_.wait_ge(sem["cspx1"], 16 * (NT // 2))
                g_.wait_ge(sem["cspi"], 16 * (NT // 2))
                g_.wait_ge(sem["cspi1"], 16 * (NT // 2))

                def scatters(em):
                    for st in range(STE):
                        T3 = (em * 4 + 3) * SGE + st // 4
                        g_.wait_ge(sem["ddve"], T3 * 8 + (st % 4) * 2 + 2)
                        g_.indirect_dma_start(
                            out=yin_h, out_offset=bass.IndirectOffsetOnAxis(
                                idxe[:, em * STE + st: em * STE + st + 1], 0),
                            in_=acc[:, st * D: st * D + D], in_offset=None,
                            compute_op=ALU.add).then_inc(sem["dsca"], 16)

                for e in range(E):
                    idx = e * 4
                    if e >= 1:
                        g_.wait_ge(sem["dpep"], 8 * SGE * (idx - 1))
                        g_.dma_start(out=wfcq[:, (idx % 2) * CC * HQ:(idx % 2 + 1) * CC * HQ]
                                     .rearrange("p (cc h) -> p cc h", cc=CC),
                                     in_=wfc_d[e * D:(e + 1) * D, 0:HQ]
                                     .rearrange("(cc p) h -> p cc h", p=128)
                                     ).then_inc(P2("dw", idx), 16)
                    for st in range(STE):
                        gst = e * STE + st
                        g_.wait_ge(sem["dic"], 2 * STE * e + 2 * st + 1)
                        if gst >= 6:
                            g_.wait_ge(P2("dg", gst), 16 * ((gst // 2) - 2))
                            g_.wait_ge(sem["dgt"], 8 * (gst - 5))
                        g_.indirect_dma_start(
                            out=x1g[:, (gst % 6) * D:(gst % 6) * D + D],
                            out_offset=None,
                            in_=x1b_h,
                            in_offset=bass.IndirectOffsetOnAxis(
                                idxe[:, e * STE + st: e * STE + st + 1], 0),
                            ).then_inc(P2("dg", gst), 16)
                    if e >= 1:
                        scatters(e - 1)
                    for hq in range(4):
                        idx = e * 4 + hq
                        if hq >= 1:
                            g_.wait_ge(sem["dpep"], 8 * SGE * (idx - 1))
                            g_.dma_start(out=wfcq[:, (idx % 2) * CC * HQ:(idx % 2 + 1) * CC * HQ]
                                         .rearrange("p (cc h) -> p cc h", cc=CC),
                                         in_=wfc_d[e * D:(e + 1) * D, hq * HQ:(hq + 1) * HQ]
                                         .rearrange("(cc p) h -> p cc h", p=128)
                                         ).then_inc(P2("dw", idx), 16)
                        if idx >= 1:
                            g_.wait_ge(sem["dpep"], 8 * SGE * idx)
                            g_.dma_start(out=wpjq[:].rearrange("p (ht c) -> p ht c", ht=HT),
                                         in_=wpj_d[e * H + hq * HQ: e * H + (hq + 1) * HQ, :]
                                         .rearrange("(ht p) c -> p ht c", p=128)
                                         ).then_inc(sem["dpk"], 16)
                scatters(E - 1)

            @blk.tensor
            def _(t):
                for e in range(E):
                    for st in range(STE):
                        gst = e * STE + st
                        t.wait_ge(P2("dg", gst), 16 * (gst // 2 + 1))
                        for cc in range(CC):
                            jj = gst * 8 + cc
                            if jj >= 4:
                                t.wait_ge(sem["dgv"], jj - 3)
                            t.transpose(PDT[jj % 4][:, 0:128],
                                        x1g[:, (gst % 6) * D + cc * 128:
                                            (gst % 6) * D + cc * 128 + 128],
                                        idb[:]).then_inc(sem["dgt"], 1)
                    for hq in range(4):
                        idx = e * 4 + hq

                        def fc(sg):
                            if hq == 0:
                                t.wait_ge(sem["dgv"], 8 * STE * e + 32 * (sg + 1))
                            T = idx * SGE + sg
                            for ht in range(HT):
                                j = T * HT + ht
                                if j >= 2:
                                    t.wait_ge(sem["dact"], j - 1)
                                for cc in range(CC):
                                    mm = t.matmul(
                                        PDH[j % 2][:, 0:512],
                                        wfcq[:, (idx % 2) * CC * HQ + cc * HQ + ht * 128:
                                             (idx % 2) * CC * HQ + cc * HQ + ht * 128 + 128],
                                        x1gT[:, cc * CAP + sg * 512: cc * CAP + sg * 512 + 512],
                                        start=(cc == 0), stop=(cc == CC - 1))
                                mm.then_inc(sem["dpeh"], 1)

                        def proj(sg):
                            T = idx * SGE + sg
                            t.wait_ge(sem["dact"], HT * (T + 1))
                            for tt in range(4):
                                for ch in range(2):
                                    p = T * 8 + tt * 2 + ch
                                    if p >= 2:
                                        t.wait_ge(sem["ddve"], p - 1)
                                    for ht in range(HT):
                                        mm = t.matmul(
                                            PDP[p % 2][:, 0:512],
                                            hTs[:, (T % 2) * HT * 512 + ht * 512 + tt * 128:
                                                (T % 2) * HT * 512 + ht * 512 + tt * 128 + 128],
                                            wpjq[:, ht * D + ch * 512: ht * D + ch * 512 + 512],
                                            start=(ht == 0), stop=(ht == HT - 1))
                                    mm.then_inc(sem["dpep"], 1)

                        t.wait_ge(P2("dw", idx), 16 * (idx // 2 + 1))
                        first_p = True
                        for kind, sg in sched:
                            if kind == "f":
                                fc(sg)
                            else:
                                if first_p:
                                    t.wait_ge(sem["dpk"], 16 * (idx + 1))
                                    first_p = False
                                proj(sg)

            @blk.scalar
            def _(s):
                s.wait_ge(sem["consts"], CONSTS)
                def pcke_load(e_):
                    s.dma_start(out=pcke[:, (e_ % 2) * STE * 2:(e_ % 2 + 1) * STE * 2]
                                .rearrange("p (t c) -> p t c", t=STE),
                                in_=pck_h[e_ * CAP: e_ * CAP + STE * 128, :]
                                .rearrange("(t p) c -> p t c", p=128)
                                ).then_inc(sem["dx1"], 16)

                for e in range(E):
                    if e == 0:
                        s.wait_ge(sem["rpk"], RPK_TOT)
                        s.wait_ge(sem["rst"], 32 * (NT // 2))
                        s.wait_ge(sem["rst1"], 32 * (NT // 2))
                        pcke_load(0)
                        pcke_load(1)
                    elif e + 1 < E:
                        s.wait_ge(sem["dic"], 2 * STE * e)
                        pcke_load(e + 1)
                    for hq in range(4):
                        idx = e * 4 + hq
                        for sg in range(SGE):
                            T = idx * SGE + sg
                            for ht in range(HT):
                                j = T * HT + ht
                                s.wait_ge(sem["dpeh"], j + 1)
                                if T >= 2:
                                    s.wait_ge(sem["dpep"], 8 * (T - 1))
                                s.activation(hTs[:, (T % 2) * HT * 512 + ht * 512:
                                                 (T % 2) * HT * 512 + ht * 512 + 512],
                                             PDH[j % 2][:, 0:512],
                                             AF.Gelu_apprx_tanh,
                                             bias=bfc_s[:, (hq * HT + ht) * 8 + e:
                                                        (hq * HT + ht) * 8 + e + 1]
                                             ).then_inc(sem["dact"], 1)

            @blk.vector
            def _(v):
                def casts(e):
                    v.wait_ge(sem["dx1"], 16 * (e + 1))
                    p0 = (e % 2) * STE * 2
                    for st in range(STE):
                        v.tensor_copy(idxe[:, e * STE + st: e * STE + st + 1],
                                      pcke[:, p0 + st * 2: p0 + st * 2 + 1]).then_inc(sem["dic"], 1)
                        v.tensor_copy(csle[:, e * STE + st: e * STE + st + 1],
                                      pcke[:, p0 + st * 2 + 1: p0 + st * 2 + 2]).then_inc(sem["dic"], 1)

                casts(0)
                for e in range(E):
                    for st in range(STE):
                        gst = e * STE + st
                        for cc in range(CC):
                            jj = gst * 8 + cc
                            v.wait_ge(sem["dgt"], jj + 1)
                            if st == 0 and cc == 0 and e >= 1:
                                v.wait_ge(sem["dpeh"], FC_PER_E * e)
                            v.tensor_copy(x1gT[:, cc * CAP + st * 128:
                                               cc * CAP + st * 128 + 128],
                                          PDT[jj % 4][:, 0:128]).then_inc(sem["dgv"], 1)
                    if e + 1 < E:
                        casts(e + 1)
                    for hq in range(4):
                        idx = e * 4 + hq
                        for sg in range(SGE):
                            T = idx * SGE + sg
                            for tt in range(4):
                                for ch in range(2):
                                    p = T * 8 + tt * 2 + ch
                                    v.wait_ge(sem["dpep"], p + 1)
                                    if hq == 0 and sg == 0 and tt == 0 and ch == 0:
                                        v.wait_ge(sem["dic"], 2 * STE * (e + 1))
                                        if e >= 1:
                                            v.wait_ge(sem["dsca"], 16 * STE * e)
                                    sl = (sg * 4 + tt) * D + ch * 512
                                    cs = csle[:, e * STE + sg * 4 + tt:
                                              e * STE + sg * 4 + tt + 1]
                                    if hq == 0:
                                        v.tensor_scalar(acc[:, sl: sl + 512],
                                                        PDP[p % 2][:, 0:512],
                                                        cs, None,
                                                        ALU.mult).then_inc(sem["ddve"], 1)
                                    else:
                                        v.scalar_tensor_tensor(
                                            acc[:, sl: sl + 512],
                                            PDP[p % 2][:, 0:512], cs,
                                            acc[:, sl: sl + 512],
                                            ALU.mult, ALU.add).then_inc(sem["ddve"], 1)

        with ExitStack() as psE:
            ec = psE.enter_context
            yinb = ec(nc.sbuf_tensor("yinb", [128, 2 * D], F32))
            outst = ec(nc.sbuf_tensor("outst", [128, 2 * D], F32))
            dlns = ec(nc.sbuf_tensor("dlns", [128, 8], F32))
            dsq = ec(nc.sbuf_tensor("dsq", [128, D], F32))
            blk = ec(nc.Block())

            @blk.sync
            def _(q):
                q.wait_ge(sem["dsca"], 16 * STE * E)
                for ti in range(NT):
                    if ti >= 2:
                        q.wait_ge(sem["dv"], 7 * (ti - 2) + 7)
                    q.dma_start(out=yinb[:, (ti % 2) * D:(ti % 2) * D + D],
                                in_=yin_h[ti * 128:(ti + 1) * 128, :]
                                ).then_inc(P2("cy", ti), 16)

            @blk.scalar
            def _(s):
                na = 0
                for ti in range(NT):
                    o2 = ti % 2
                    s.wait_ge(sem["dv"], 7 * ti + 6)
                    s.activation(dlns[:, o2 * 4 + 3: o2 * 4 + 4],
                                 dlns[:, o2 * 4 + 1: o2 * 4 + 2],
                                 AF.Ln, bias=cns[:, 1:2]).then_inc(sem["da"], 1)
                    na += 1
                    s.wait_ge(sem["da"], na)
                    s.activation(dlns[:, o2 * 4 + 3: o2 * 4 + 4],
                                 dlns[:, o2 * 4 + 3: o2 * 4 + 4],
                                 AF.Exp, scale=-0.5).then_inc(sem["drstd"], 1)

            @blk.vector
            def _(v):
                nv = 0

                def W():
                    v.wait_ge(sem["dv"], nv)

                def V(instr):
                    nonlocal nv
                    instr.then_inc(sem["dv"], 1)
                    nv += 1

                for ti in range(NT):
                    o2 = ti % 2
                    c0 = o2 * 4
                    at = yinb[:, o2 * D: o2 * D + D]
                    v.wait_ge(P2("cy", ti), 16 * (NT // 2 + ti // 2 + 1))
                    W()
                    V(v.reduce_sum(dlns[:, c0 + 0: c0 + 1], at, axis=AX))
                    W()
                    V(v.tensor_mul(dsq[:, 0:D], at, at))
                    W()
                    V(v.reduce_sum(dlns[:, c0 + 1: c0 + 2], dsq[:, 0:D], axis=AX))
                    W()
                    V(v.tensor_scalar(dlns[:, c0 + 2: c0 + 3],
                                      dlns[:, c0 + 0: c0 + 1],
                                      1.0 / D, None, ALU.mult))
                    W()
                    V(v.tensor_mul(dlns[:, c0 + 0: c0 + 1],
                                   dlns[:, c0 + 2: c0 + 3],
                                   dlns[:, c0 + 2: c0 + 3]))
                    W()
                    V(v.scalar_tensor_tensor(dlns[:, c0 + 1: c0 + 2],
                                             dlns[:, c0 + 1: c0 + 2],
                                             1.0 / D,
                                             dlns[:, c0 + 0: c0 + 1],
                                             ALU.mult, ALU.subtract))
                    v.wait_ge(sem["drstd"], ti + 1)
                    if ti >= 2:
                        v.wait_ge(P2("dout", ti), 16 * (ti // 2))
                    W()
                    V(v.tensor_scalar(outst[:, o2 * D: o2 * D + D], at,
                                      dlns[:, c0 + 2: c0 + 3],
                                      dlns[:, c0 + 3: c0 + 4],
                                      ALU.subtract, ALU.mult))

            @blk.gpsimd
            def _(g_):
                for ti in range(NT):
                    g_.wait_ge(sem["dv"], 7 * ti + 7)
                    g_.dma_start(out=out_d[ti * 128:(ti + 1) * 128, :],
                                 in_=outst[:, (ti % 2) * D:(ti % 2) * D + D]
                                 ).then_inc(P2("dout", ti), 16)
                g_.wait_ge(sem["dout"], 16 * (NT // 2))
                g_.wait_ge(sem["dout1"], 16 * (NT // 2))
    return nc


def kernel(L0_attn_x, L0_attn_weight_qkv, L0_attn_out_w, L0_attn_out_b,
           L0_ffn_router_w, L0_ffn_w_fc, L0_ffn_b_fc, L0_ffn_w_proj, L0_ffn_b_proj):
    x = np.ascontiguousarray(np.asarray(L0_attn_x, dtype=np.float32))
    B, seq, d = x.shape
    wqkv = np.ascontiguousarray(np.asarray(L0_attn_weight_qkv, dtype=np.float32))
    wo = np.ascontiguousarray(np.asarray(L0_attn_out_w, dtype=np.float32))
    ob = np.ascontiguousarray(np.asarray(L0_attn_out_b, dtype=np.float32)).reshape(1, d)
    rw = np.ascontiguousarray(np.asarray(L0_ffn_router_w, dtype=np.float32))
    wfc = np.ascontiguousarray(np.asarray(L0_ffn_w_fc, dtype=np.float32))
    bfc = np.ascontiguousarray(np.asarray(L0_ffn_b_fc, dtype=np.float32))
    wpj = np.ascontiguousarray(np.asarray(L0_ffn_w_proj, dtype=np.float32))
    bpj = np.ascontiguousarray(np.asarray(L0_ffn_b_proj, dtype=np.float32))
    Ex, _, H = wfc.shape
    wfc2 = wfc.reshape(Ex * d, H)
    wpj2 = wpj.reshape(Ex * H, d)
    idb = np.eye(128, dtype=ml_dtypes.bfloat16)
    idf = np.eye(128, dtype=np.float32)

    nc = build(seq=seq, H=H)
    in_maps = []
    for b in range(B):
        in_maps.append({
            "L0_attn_x": x[b],
            "L0_attn_weight_qkv": wqkv,
            "L0_attn_out_w": wo,
            "L0_attn_out_b": ob,
            "L0_ffn_router_w": rw,
            "L0_ffn_w_fc": wfc2,
            "L0_ffn_b_fc": bfc,
            "L0_ffn_w_proj": wpj2,
            "L0_ffn_b_proj": bpj,
            "ident_bf": idb,
            "ident_f32": idf,
        })
    res = run_bass_kernel_spmd(nc, in_maps, core_ids=list(range(B)))
    return np.stack([r["out"] for r in res.results], axis=0)
